# version 08
# baseline (speedup 1.0000x reference)
import numpy as np
from collections import defaultdict
from contextlib import ExitStack

import concourse.bass as bass
import concourse.mybir as mybir
from concourse.bass_utils import run_bass_kernel_spmd

F32 = mybir.dt.float32
BF16 = mybir.dt.bfloat16
I32 = mybir.dt.int32
AF = mybir.ActivationFunctionType
ALU = mybir.AluOpType
AXX = mybir.AxisListType.X

H = 1024
E = 2048
E2 = 4096
NL = 8
RK = 64
TOK = 2048
NTC = 16
SEQ = 4096
TC = 127
NCH = 33
MIDS = 64
EPS = 1e-5

ENGS = ("sync", "scalar", "vector", "tensor", "gpsimd")
QSEM = {"sync": "dy", "scalar": "dc", "gpsimd": "dg"}


class Build:
    def __init__(self):
        self.ops = {e: [] for e in ENGS}
        self.tot = defaultdict(int)

    def emit(self, eng, fn, waits=(), inc=None, amt=1):
        self.ops[eng].append((fn, tuple(waits), inc, amt))
        if inc is not None:
            self.tot[inc] += amt

    def v(self, sem):
        return self.tot[sem]


class _StopBuild(Exception):
    pass


def build_nc(upto=99, skip_ffn=False, detect_races=False, p4skip=(), debug=False):
    nc = bass.Bass("TRN2", target_bir_lowering=False, dynamic_dma_scratch_size=8192,
                   detect_race_conditions=detect_races)
    B = Build()

    def din(name, shape, dt=F32):
        return nc.dram_tensor(name, shape, dt, kind="ExternalInput").ap()

    x_loc = din("x_loc", [TOK, H])
    x_halo = din("x_halo", [3, H])
    haloflag = din("haloflag", [1, 1])
    ln_w = din("ln_w", [1, H]); ln_b = din("ln_b", [1, H])
    ipw_r = din("ipw_r", [32, 128, 1024])
    ipb_col = din("ipb_col", [128, 16]); ipb_row = din("ipb_row", [1, E2])
    cw_col = din("cw_col", [128, 64]); cb_col = din("cb_col", [128, 16])
    xpw_r = din("xpw_r", [128, 16 * 96])
    dpw = din("dpw", [RK, E]); dpb_row = din("dpb_row", [1, E])
    a_logt = din("a_logt", [NL, E])
    d_row = din("d_row", [1, E])
    opw_d = din("opw", [E, H]); opb_row = din("opb_row", [1, H])
    gwt_r = din("gwt_r", [128, 64]); lb = din("lb", [8, 1])
    egwt_r = din("egwt_r", [8, 128, 4096]); euwt_r = din("euwt_r", [8, 128, 4096])
    edwt_r = din("edwt_r", [8, 128, 4096])
    sgwt_r = din("sgwt_r", [128, 8192]); suwt_r = din("suwt_r", [128, 8192])
    sdwt_r = din("sdwt_r", [128, 8192])
    bcperm = din("bcperm", [32, 32])
    alphab = din("alphab", [1, 4])

    out_loc = nc.dram_tensor("out_loc", [TOK, H], F32, kind="ExternalOutput").ap()
    dbg = {}
    dbg_sc = nc.dram_tensor("dbg_sc", [8, TOK], F32, kind="ExternalOutput").ap()
    dbg_sct = nc.dram_tensor("dbg_sct", [128, NTC * 8], F32, kind="ExternalOutput").ap()
    dbg_wts = nc.dram_tensor("dbg_wts", [128, NTC * 16], F32, kind="ExternalOutput").ap()

    ag_in = nc.dram_tensor("ag_in", [TOK, E2], BF16, kind="Internal").ap()
    ag_out = nc.dram_tensor("ag_out", [SEQ, E2], BF16, kind="Internal").ap()
    agbc_in = nc.dram_tensor("agbc_in", [TOK, 32], BF16, kind="Internal").ap()
    agbc_out = nc.dram_tensor("agbc_out", [SEQ, 32], BF16, kind="Internal").ap()
    y_in = nc.dram_tensor("y_in", [SEQ, E], BF16, kind="Internal").ap()
    y_out = nc.dram_tensor("y_out", [TOK, E], BF16, kind="Internal").ap()
    r_in = nc.dram_tensor("r_in", [1, 16], F32, kind="Internal").ap()
    r_out = nc.dram_tensor("r_out", [1, 16], F32, kind="Internal").ap()
    sg_dram = nc.dram_tensor("sg_dram", [TOK, E], BF16, kind="Internal").ap()
    dcm_dram = nc.dram_tensor("dcm_dram", [1, E], BF16, kind="Internal").ap()
    wm_dram = nc.dram_tensor("wm_dram", [1, 8], F32, kind="Internal").ap()
    swt_dram = nc.dram_tensor("swt_dram", [8, TOK], BF16, kind="Internal").ap()
    apos_dram = nc.dram_tensor("apos_dram", [NL, E], BF16, kind="Internal").ap()
    xres_dram = nc.dram_tensor("xres_dram", [TOK, H], F32, kind="Internal").ap()

    with ExitStack() as ctx:
        SB = lambda name, shape, dt: ctx.enter_context(nc.sbuf_tensor(name, shape, dt))
        xres = SB("xres", [128, NTC * H], F32)
        buf_fm = SB("buf_fm", [128, 8 * TOK], BF16)
        arn_c = SB("arn_c", [128, 8 * E], BF16)
        ar_a = SB("ar_a", [128, 8 * E], BF16)
        ar_b = SB("ar_b", [128, 8 * E], BF16)
        araf = ar_a.ap().bitcast(F32)
        xrb = xres.ap().bitcast(BF16)
        lnw_bc = araf[:, 2048:3072]
        lnb_bc = araf[:, 3072:4096]
        opb_bc = araf[:, 4096:5120]
        iota_p = SB("iota_p", [128, 128], I32)
        iota_f = SB("iota_f", [128, 128], I32)
        tric = SB("tric", [128, 128], BF16)
        trip = SB("trip", [128, 128], BF16)
        identb = SB("identb", [128, 128], BF16)
        trif32 = SB("trif32", [128, 128], F32)
        ones128f = SB("ones128f", [128, 1], F32)
        ones1b = SB("ones1b", [1, 128], BF16)
        ones8 = SB("ones8", [128, 8], F32)
        ipb_sb = SB("ipb_sb", [128, 16], F32)
        cw_sb = SB("cw_sb", [128, 64], F32)
        cb_sb = SB("cb_sb", [128, 16], F32)
        dpw_sb = xrb[0:RK, 0:2048]
        dtr_sbv = xrb[0:RK, 2048:4096]
        bc_allv = xrb[0:32, 4096:6144]
        ip_sbv = xrb[:, 6144:8192]
        wg_sbv = xrb[:, 8192:12288]
        dpb_sb = xrb[0:1, 12288:14336]
        ipbg_row = xrb[0:1, 14336:16384]
        xpw_sbv = xrb[:, 16384:17920]
        dhalf_bc = xrb[:, 17920:19968]
        aposv = xrb[0:NL, 19968:22016]
        bcperm_sb = SB("bcperm_sb", [32, 32], BF16)
        ab_col = SB("ab_col", [128, 4], F32)
        gw_hi = SB("gw_hi", [128, 64], BF16)
        gw_lo = SB("gw_lo", [128, 64], BF16)
        gw_f = SB("gw_f", [128, 64], F32)
        lb_sb = SB("lb_sb", [8, 1], F32)
        scT = SB("scT", [128, NTC * 8], F32)
        smf = SB("smf", [128, 128], F32)
        wtsel = SB("wtsel", [128, NTC * 16], F32)
        wm_bc = SB("wm_bc", [128, 8], F32)
        selw8 = SB("selw8", [128, 8], BF16)
        swt_st = SB("swt_st", [8, 128], BF16)
        xch = SB("xch", [128, H], F32)
        f32t = SB("f32t", [128, 3 * H], F32)
        stat = SB("stat", [128, 16], F32)
        halo3 = araf[:, 5120:6144][0:3, :]
        halo_fm = SB("halo_fm", [128, 32], BF16)
        flag_c = SB("flag_c", [128, 1], F32)
        xpreh = SB("xpreh", [128, 64], BF16)
        rres = SB("rres", [1, 16], F32)
        bcst = SB("bcst", [128, 32], BF16)
        scoresS = f32t.ap()[0:8, 0:E]
        sg1p4 = xres.ap()[:, 11264:13312]
        PS0 = ctx.enter_context(nc.psum_tensor("PS0", [128, 2048], F32))
        PS1 = ctx.enter_context(nc.psum_tensor("PS1", [128, 2048], F32))

        sems = {}
        for s in ("dg", "dy", "dc", "cv", "cs", "ct", "cg", "cc"):
            sems[s] = ctx.enter_context(nc.semaphore("sem_" + s))

        blk = ctx.enter_context(nc.Block())

        def _ck(k):
            if upto <= k:
                raise _StopBuild

        PS0b = PS0.ap().bitcast(BF16)
        PS1b = PS1.ap().bitcast(BF16)

        def dma(q, out, in_, waits=()):
            B.emit(q, lambda e, o=out, i=in_: e.dma_start(out=o, in_=i), waits, QSEM[q], 16)

        def vop(fn, waits=(), inc="cv", drain=False):
            B.emit("vector", fn, waits, inc)
            if drain:
                B.emit("vector", lambda e: e.drain(), (), None)

        def sop(fn, waits=(), inc="cs"):
            B.emit("scalar", fn, waits, inc)

        def top(fn, waits=(), inc=None):
            B.emit("tensor", fn, waits, inc)

        def gop(fn, waits=(), inc="cg"):
            B.emit("gpsimd", fn, waits, inc)

        partial = [False]
        try:
            pass
        except Exception:
            pass
        try:
            dma("sync", lnw_bc, ln_w[0, :].partition_broadcast(128))
            dma("sync", lnb_bc, ln_b[0, :].partition_broadcast(128))
            dma("sync", opb_bc, opb_row[0, :].partition_broadcast(128))
            dma("sync", flag_c[:, :], haloflag[0, 0:1].partition_broadcast(128))
            dma("sync", ab_col[:, :], alphab[0, :].partition_broadcast(128))
            dma("sync", halo3, x_halo[:, :])
            dma("sync", ipb_sb[:, :], ipb_col[:, :])
            dma("sync", cw_sb[:, :], cw_col[:, :])
            dma("sync", cb_sb[:, :], cb_col[:, :])
            dma("sync", f32t[0:NL, 0:E], a_logt[:, :])
            dma("gpsimd", bcperm_sb[:, :], bcperm[:, :])
            dma("gpsimd", gw_f[:, :], gwt_r[:, :])
            dma("gpsimd", lb_sb[:, :], lb[:, :])
            w0g, w0y = B.v("dg"), B.v("dy")
            gop(lambda e: e.iota(iota_p[:, :], pattern=[[0, 128]], base=0, channel_multiplier=1))
            gop(lambda e: e.iota(iota_f[:, :], pattern=[[1, 128]], base=0, channel_multiplier=0))
            iota_g = B.v("cg")
            vop(lambda e: e.tensor_tensor(trif32[:, :], iota_p[:, :], iota_f[:, :], op=ALU.is_le),
                waits=[("cg", iota_g)])
            vop(lambda e: e.tensor_scalar(tric[:, :], iota_p[:, :], float(MIDS), None, op0=ALU.is_le))
            vop(lambda e: e.tensor_tensor(tric[:, :], trif32[:, :], tric[:, :], op=ALU.subtract))
            vop(lambda e: e.tensor_copy(trip[:, :], trif32[:, :]))
            vop(lambda e: e.tensor_tensor(identb[:, :], iota_p[:, :], iota_f[:, :], op=ALU.is_equal))
            vop(lambda e: e.memset(ones128f[:, :], 1.0))
            vop(lambda e: e.memset(ones1b[:, :], 1.0))
            vop(lambda e: e.memset(ones8[:, :], 1.0))
            sop(lambda e: e.activation(ar_b[0:NL, 0:E], f32t[0:NL, 0:E], AF.Exp),
                waits=[("dy", w0y)])
            apos_s = B.v("cs")
            dma("scalar", apos_dram[:, :], ar_b[0:NL, 0:E], waits=[("cs", apos_s)])
            apos_d = B.v("dc")
            vop(lambda e: e.tensor_copy(gw_hi[:, :], gw_f[:, :]), waits=[("dg", w0g)])
            vop(lambda e: e.tensor_tensor(gw_lo[:, :], gw_f[:, :], gw_hi[:, :], op=ALU.subtract))
            p0_v, p0_s = B.v("cv"), B.v("cs")
            _ck(0)

            def emit_ln(src, nrow, xn_dst, hs_dst, w=()):
                s0 = stat[0:nrow, 0:1]; s1 = stat[0:nrow, 1:2]; s2 = stat[0:nrow, 2:3]
                sq = f32t[0:nrow, 2 * H:3 * H]
                sop(lambda e: e.activation(sq, src, AF.Square), waits=w)
                vop(lambda e: e.reduce_sum(s0, src, axis=AXX), waits=w, drain=True)
                vop(lambda e: e.reduce_sum(s1, sq, axis=AXX), waits=[("cs", B.v("cs"))])
                vop(lambda e: e.tensor_scalar(s0, s0, 1.0 / H, None, op0=ALU.mult))
                vop(lambda e: e.tensor_scalar(s1, s1, 1.0 / H, None, op0=ALU.mult))
                vop(lambda e: e.tensor_tensor(s2, s0, s0, op=ALU.mult))
                vop(lambda e: e.tensor_tensor(s1, s1, s2, op=ALU.subtract))
                vop(lambda e: e.tensor_scalar(s1, s1, EPS, None, op0=ALU.add))
                sop(lambda e: e.activation(s2, s1, AF.Sqrt), waits=[("cv", B.v("cv"))])
                vop(lambda e: e.reciprocal(s2, s2), waits=[("cs", B.v("cs"))])
                vop(lambda e: e.tensor_scalar(xn_dst, src, s0, s2, op0=ALU.subtract, op1=ALU.mult))
                sop(lambda e: e.activation(sq, xn_dst, AF.Square), waits=[("cv", B.v("cv"))])
                vop(lambda e: e.reduce_sum(s0, xn_dst, axis=AXX))
                vop(lambda e: e.reduce_sum(s1, sq, axis=AXX), waits=[("cs", B.v("cs"))])
                vop(lambda e: e.tensor_scalar(s0, s0, 1.0 / H, None, op0=ALU.mult))
                vop(lambda e: e.tensor_scalar(s1, s1, 1.0 / H, None, op0=ALU.mult))
                vop(lambda e: e.tensor_tensor(s2, s0, s0, op=ALU.mult))
                vop(lambda e: e.tensor_tensor(s1, s1, s2, op=ALU.subtract))
                vop(lambda e: e.tensor_scalar(s1, s1, EPS, None, op0=ALU.add))
                sop(lambda e: e.activation(s2, s1, AF.Sqrt), waits=[("cv", B.v("cv"))])
                vop(lambda e: e.reciprocal(s2, s2), waits=[("cs", B.v("cs"))])
                vop(lambda e: e.tensor_scalar(hs_dst, xn_dst, s0, s2, op0=ALU.subtract, op1=ALU.mult))
                vop(lambda e: e.tensor_tensor(hs_dst, hs_dst, lnw_bc[0:nrow, :], op=ALU.mult))
                vop(lambda e: e.tensor_tensor(hs_dst, hs_dst, lnb_bc[0:nrow, :], op=ALU.add))

            emit_ln(halo3[:, :], 3, f32t[0:3, 0:H], f32t[0:3, H:2 * H],
                    w=[("dy", w0y), ("cv", p0_v), ("cs", p0_s)])
            vop(lambda e: e.tensor_scalar(f32t[0:3, H:2 * H], f32t[0:3, H:2 * H],
                                          flag_c[0:3, 0:1], None, op0=ALU.mult))
            vop(lambda e: e.tensor_copy(ar_a[0:3, 0:H], f32t[0:3, H:2 * H]))
            hv = B.v("cv")
            for j in range(8):
                top(lambda e, j=j: e.transpose(PS0b[:, j * 4:j * 4 + 3],
                                               ar_a[0:3, j * 128:(j + 1) * 128], identb[0:3, 0:3]),
                    waits=[("cv", hv)], inc=("ct" if j == 7 else None))
            vop(lambda e: e.tensor_copy(
                halo_fm[:, 0:24].rearrange("p (j k) -> p j k", j=8),
                PS0b[:, 0:32].rearrange("p (j k) -> p j k", j=8)[:, :, 0:3]),
                waits=[("ct", B.v("ct"))])
            halo_done_v = B.v("cv")

            evac_marks = [halo_done_v]
            for tc in range(NTC):
                xc_ap = xch[:, :]
                wv = [("cv", B.v("cv"))]
                dma("sync", xc_ap, x_loc[tc * 128:(tc + 1) * 128, :], waits=wv)
                wld = [("dy", B.v("dy")), ("cv", p0_v), ("cs", p0_s)]
                xn_f = f32t[:, 0:H]
                hs_f = f32t[:, H:2 * H]
                emit_ln(xc_ap, 128, xn_f, hs_f, w=wld)
                vop(lambda e, xc_ap=xc_ap, tc=tc: e.tensor_tensor(
                    xres[:, tc * H:(tc + 1) * H], xc_ap, xn_f, op=ALU.add))
                vop(lambda e, tc=tc: e.tensor_tensor(
                    xres[:, tc * H:(tc + 1) * H], xres[:, tc * H:(tc + 1) * H],
                    opb_bc[:, :], op=ALU.add))
                vop(lambda e: e.tensor_copy(ar_a[:, 0:H], xn_f), waits=[("ct", B.v("ct"))])
                vop(lambda e: e.tensor_copy(ar_a[:, H:2 * H], hs_f))
                vop(lambda e: e.tensor_tensor(ar_a[:, 2 * H:3 * H], xn_f, ar_a[:, 0:H],
                                              op=ALU.subtract))
                cast_v = B.v("cv")
                for so, po in ((0, 0), (H, 1024), (2 * H, 2048)):
                    for j in range(8):
                        top(lambda e, so=so, po=po, j=j: e.transpose(
                            PS0b[:, po + j * 128:po + (j + 1) * 128],
                            ar_a[:, so + j * 128:so + (j + 1) * 128], identb[:, :]),
                            waits=[("cv", cast_v)],
                            inc=("ct" if (so == 2 * H and j == 7) else None))
                tp_done = B.v("ct")
                for po, dstt in ((0, arn_c), (1024, buf_fm), (2048, ar_b)):
                    ew = [("ct", tp_done)] + ([("dc", apos_d)] if tc == 0 else [])
                    vop(lambda e, po=po, dstt=dstt, tc=tc: e.tensor_copy(
                        dstt[:, :].rearrange("q (j t) -> q j t", j=8)[:, :, tc * 128:(tc + 1) * 128],
                        PS0b[:, po:po + 1024].rearrange("q (j t) -> q j t", j=8)),
                        waits=ew)
                evac_marks.append(B.v("cv"))
            p1_v, p1_s, p1_t, p1_dy = B.v("cv"), B.v("cs"), B.v("ct"), B.v("dy")
            _ck(1)
            xn_fm, hs_fm, xnlo_fm = arn_c, buf_fm, ar_b

            for tt in range(4):
                combos = [(xn_fm, gw_hi), (xn_fm, gw_lo), (xnlo_fm, gw_hi)]
                n_mm = len(combos) * 8
                k = 0
                for xa, wa in combos:
                    for j in range(8):
                        top(lambda e, xa=xa, wa=wa, j=j, tt=tt, k=k, n=n_mm: e.matmul(
                            PS1[0:8, tt * 512:(tt + 1) * 512],
                            wa[:, j * 8:(j + 1) * 8],
                            xa[:, :].rearrange("q (jj t) -> q jj t", jj=8)[:, j, tt * 512:(tt + 1) * 512],
                            start=(k == 0), stop=(k == n_mm - 1)),
                            waits=[("cv", p1_v)],
                            inc=("ct" if k == n_mm - 1 else None))
                        k += 1
            sop(lambda e: e.activation(scoresS[:, :], PS1[0:8, :], AF.Sigmoid, bias=lb_sb[:, :]),
                waits=[("ct", B.v("ct")), ("dg", w0g)])
            sc_s = B.v("cs")
            if debug:
                dma("sync", dbg_sc, scoresS, waits=[("cs", sc_s)])
            idf32 = trif32
            vop(lambda e: e.tensor_copy(trif32[:, :], identb[:, :]), waits=[("ct", p1_t)])
            idf_v = B.v("cv")
            for tc in range(NTC):
                top(lambda e, tc=tc: e.transpose(PS0[:, tc * 8:(tc + 1) * 8],
                                                 scoresS[:, tc * 128:(tc + 1) * 128],
                                                 trif32[0:8, 0:8]),
                    waits=[("cs", sc_s), ("cv", idf_v)],
                    inc=("ct" if tc == NTC - 1 else None))
            vop(lambda e: e.tensor_copy(scT[:, :], PS0[:, 0:NTC * 8]), waits=[("ct", B.v("ct"))])
            scT_v = B.v("cv")
            if debug:
                dma("sync", dbg_sct, scT[:, :], waits=[("cv", scT_v)])

            epsc = stat[:, 8:9]
            vop(lambda e: e.memset(epsc, 1e-8), waits=[("cv", scT_v)], drain=True)
            for tc in range(NTC):
                sc = scT[:, tc * 8:(tc + 1) * 8]
                mx8 = smf[:, 0:8]; mrt = smf[:, 8:16]; diff = smf[:, 16:24]
                den = stat[:, 9:10]; den2 = stat[:, 10:11]; rec = stat[:, 11:12]
                vop(lambda e, sc=sc: e.max(mx8, sc), drain=True)
                vop(lambda e: e.memset(mx8[:, 2:8], -1.0), drain=True)
                vop(lambda e, sc=sc: e.match_replace(out=mrt, in_to_replace=mx8,
                                                     in_values=sc, imm_value=0.0),
                    drain=True)
                vop(lambda e, sc=sc: e.tensor_tensor(diff, sc, mrt, op=ALU.subtract),
                    drain=True)
                vop(lambda e: e.tensor_tensor(den, mx8[:, 0:1], mx8[:, 1:2], op=ALU.add),
                    drain=True)
                vop(lambda e: e.tensor_tensor(den2, den, epsc, op=ALU.add), drain=True)
                vop(lambda e: e.reciprocal(rec, den2), drain=True)
                vop(lambda e, tc=tc: e.tensor_scalar(
                    wtsel[:, tc * 16:tc * 16 + 8], diff, rec, None, op0=ALU.mult),
                    drain=True)
                vop(lambda e: e.tensor_scalar(smf[:, 24:32], diff, 1e20, None, op0=ALU.mult),
                    drain=True)
                vop(lambda e, tc=tc: e.tensor_scalar(
                    wtsel[:, tc * 16 + 8:tc * 16 + 16], smf[:, 24:32], 1.0, None,
                    op0=ALU.min), drain=True)
            wts_v = B.v("cv")
            if debug:
                dma("sync", dbg_wts, wtsel[:, :], waits=[("cv", wts_v)])
            for tc in range(NTC):
                top(lambda e, tc=tc: e.matmul(
                    PS0[0:1, 256 + tc * 16:256 + (tc + 1) * 16], ones128f[:, :],
                    wtsel[:, tc * 16:(tc + 1) * 16],
                    start=True, stop=True),
                    waits=[("cv", wts_v)], inc=("ct" if tc == NTC - 1 else None))
            vop(lambda e: e.tensor_reduce(
                rres[:, :],
                PS0[0:1, 256:512].rearrange("p (t k) -> p t k", t=16).transpose([0, 2, 1]),
                axis=AXX, op=ALU.add), waits=[("ct", B.v("ct"))], drain=True)
            dma("sync", r_in[:, :], rres[:, :], waits=[("cv", B.v("cv"))])
            gop(lambda e: e.collective_compute(
                "AllReduce", ALU.add, replica_groups=[[0, 1, 2, 3, 4, 5, 6, 7]],
                ins=[r_in.opt()], outs=[r_out.opt()]),
                waits=[("dy", B.v("dy"))], inc="cc")
            ar_done = B.v("cc")
            dma("sync", rres[:, :], r_out[:, :], waits=[("cc", ar_done)])
            vop(lambda e: e.tensor_scalar(stat[0:1, 12:13].transpose([0, 1]) if False
                                          else stat[0:1, 0:8], rres[0:1, 8:16], 1.0, None,
                                          op0=ALU.max),
                waits=[("dy", B.v("dy"))], drain=True)
            vop(lambda e: e.reciprocal(stat[0:1, 8:16], stat[0:1, 0:8]), drain=True)
            vop(lambda e: e.tensor_tensor(rres[0:1, 0:8], rres[0:1, 0:8], stat[0:1, 8:16],
                                          op=ALU.mult), drain=True)
            dma("scalar", wm_dram[:, :], rres[0:1, 0:8], waits=[("cv", B.v("cv"))])
            dma("sync", wm_bc[:, :], wm_dram[0, :].partition_broadcast(128),
                waits=[("dc", B.v("dc"))])
            wm_dy = B.v("dy")
            for tc in range(NTC):
                vop(lambda e, tc=tc: e.tensor_tensor(
                    selw8[:, :], wtsel[:, tc * 16 + 8:tc * 16 + 16], wm_bc[:, :], op=ALU.mult),
                    waits=[("dy", wm_dy)] + ([("ct", B.v("ct"))] if tc > 0 else []))
                svv = B.v("cv")
                top(lambda e, tc=tc: e.transpose(PS0b[0:8, 4096 - 128:4096], selw8[:, :],
                                                 identb[:, :]),
                    waits=[("cv", svv)], inc="ct")
                vop(lambda e: e.tensor_copy(swt_st[:, :], PS0b[0:8, 4096 - 128:4096]),
                    waits=[("ct", B.v("ct")), ("dc", B.v("dc"))])
                dma("scalar", swt_dram[:, tc * 128:(tc + 1) * 128], swt_st[:, :],
                    waits=[("cv", B.v("cv"))])
            swt_dc = B.v("dc")
            p2_v = B.v("cv")
            _ck(2)

            selw_bc = ar_b.ap()[:, 12288:14336]
            exw_g = lambda c, j: ar_b[:, (c * 8 + j) * 128:(c * 8 + j) * 128 + 128]
            exw_u = lambda c, j: ar_b[:, 4096 + (c * 8 + j) * 128:4096 + (c * 8 + j) * 128 + 128]
            exd_s = lambda c, half: ar_b[:, 8192 + c * 1024 + half * 512: 8192 + c * 1024 + half * 512 + 512]
            h_t = lambda ic, t0, t1: ar_a[:, ic * 2048 + t0:ic * 2048 + t1]

            last_down_t = B.v("ct")
            last_h_v = p2_v
            for ex in ([] if skip_ffn else range(10)):
                real = ex < 8
                wld_waits = [("ct", last_down_t), ("cv", p2_v)]
                if real:
                    src_g, src_u, src_d = egwt_r[ex], euwt_r[ex], edwt_r[ex]
                else:
                    half = ex - 8
                    src_g = sgwt_r[:, half * 4096:(half + 1) * 4096]
                    src_u = suwt_r[:, half * 4096:(half + 1) * 4096]
                    src_d = sdwt_r[:, half * 4096:(half + 1) * 4096]
                dma("gpsimd", ar_b[:, 0:4096], src_g, waits=wld_waits)
                dma("gpsimd", ar_b[:, 4096:8192], src_u, waits=wld_waits)
                dma("gpsimd", ar_b[:, 8192:12288], src_d, waits=wld_waits)
                if real:
                    dma("gpsimd", selw_bc[:, :], swt_dram[ex, :].partition_broadcast(128),
                        waits=[("dc", swt_dc), ("cv", last_h_v)])
                wl = B.v("dg")
                for ic in range(4):
                    for tt in range(4):
                        gq = PS0[:, (tt % 2) * 1024:(tt % 2) * 1024 + 512]
                        uq = PS0[:, (tt % 2) * 1024 + 512:(tt % 2) * 1024 + 1024]
                        mmw = [("dg", wl), ("cv", B.v("cv"))]
                        for j in range(8):
                            top(lambda e, gq=gq, ic=ic, j=j, tt=tt: e.matmul(
                                gq, exw_g(ic, j),
                                xn_fm[:, :].rearrange("q (jj t) -> q jj t", jj=8)[:, j, tt * 512:(tt + 1) * 512],
                                start=(j == 0), stop=(j == 7)), waits=mmw)
                        for j in range(8):
                            top(lambda e, uq=uq, ic=ic, j=j, tt=tt: e.matmul(
                                uq, exw_u(ic, j),
                                xn_fm[:, :].rearrange("q (jj t) -> q jj t", jj=8)[:, j, tt * 512:(tt + 1) * 512],
                                start=(j == 0), stop=(j == 7)),
                                waits=mmw, inc=("ct" if j == 7 else None))
                        guv = B.v("ct")
                        sop(lambda e, gq=gq: e.activation(f32t[:, 0:512], gq, AF.Silu),
                            waits=[("ct", guv), ("cv", B.v("cv"))])
                        vop(lambda e, uq=uq, ic=ic, tt=tt: e.tensor_tensor(
                            h_t(ic, tt * 512, (tt + 1) * 512), f32t[:, 0:512], uq, op=ALU.mult),
                            waits=[("cs", B.v("cs")), ("ct", last_down_t)])
                        if real:
                            vop(lambda e, ic=ic, tt=tt: e.tensor_tensor(
                                h_t(ic, tt * 512, (tt + 1) * 512),
                                h_t(ic, tt * 512, (tt + 1) * 512),
                                selw_bc[:, tt * 512:(tt + 1) * 512], op=ALU.mult),
                                waits=[("dg", wl)])
                last_h_v = B.v("cv")
                for tc in range(NTC):
                    dq = PS1[:, (tc % 2) * 1024:(tc % 2) * 1024 + 1024]
                    mmw = [("cv", last_h_v if tc == 0 else B.v("cv"))]
                    k = 0
                    for ic in range(4):
                        for hh in range(2):
                            top(lambda e, dq=dq, ic=ic, hh=hh, tc=tc, k=k: e.matmul(
                                dq[:, hh * 512:(hh + 1) * 512],
                                h_t(ic, tc * 128, (tc + 1) * 128),
                                exd_s(ic, hh),
                                start=(ic == 0), stop=(ic == 3)),
                                waits=mmw, inc=("ct" if k == 7 else None))
                            k += 1
                    dn = B.v("ct")
                    vop(lambda e, dq=dq, tc=tc: e.tensor_tensor(
                        xres[:, tc * H:(tc + 1) * H], dq, xres[:, tc * H:(tc + 1) * H],
                        op=ALU.add), waits=[("ct", dn)])
                last_down_t = B.v("ct")
            p3_v, p3_t = B.v("cv"), B.v("ct")
            _ck(3)
            if "spill" not in p4skip:
                for tc in range(NTC):
                    dma("sync", xres_dram[tc * 128:(tc + 1) * 128, :],
                        xres[:, tc * H:(tc + 1) * H], waits=[("cv", p3_v)])
            spill_dy = B.v("dy")
            if "wl1" not in p4skip:
                dma("gpsimd", dpw_sb, dpw[:, :], waits=[("dy", spill_dy)])
            if "wl2" not in p4skip:
                dma("gpsimd", dpb_sb, dpb_row[:, :], waits=[("dy", spill_dy)])
                dma("gpsimd", ipbg_row, ipb_row[0:1, E:E2], waits=[("dy", spill_dy)])
            if "wl3" not in p4skip:
                dma("gpsimd", xpw_sbv, xpw_r[:, :], waits=[("dy", spill_dy)])
            p4w_dg = B.v("dg")
            _ck(3.05)

            xpre_s = arn_c[:, 0:2048]
            convT = arn_c[:, 2048:4096]
            cat6 = arn_c[:, 4096:4160]
            dtst = arn_c[:, 6144:8192]
            xctst = arn_c[:, 8192:10240]

            def xc_t(ec, t0, t1):
                if ec < 8:
                    return ar_a[:, ec * 2048 + t0:ec * 2048 + t1]
                return ar_b[:, (ec - 8) * 2048 + t0:(ec - 8) * 2048 + t1]

            for ec in range(16):
                p = ec % 2
                ipd = ip_sbv[:, p * 1024:(p + 1) * 1024]
                dma("gpsimd", ipd, ipw_r[ec],
                    waits=[("ct", B.v("ct")), ("cv", (p3_v if ec == 0 else B.v("cv"))),
                           ("dy", spill_dy)])
                wl = B.v("dg")
                for j in (range(8) if "halo" not in p4skip else []):
                    top(lambda e, ipd=ipd, j=j: e.matmul(
                        PS0[:, 0:3], ipd[:, j * 128:(j + 1) * 128],
                        halo_fm[:, j * 3:j * 3 + 3], start=(j == 0), stop=(j == 7)),
                        waits=[("dg", wl), ("cv", B.v("cv")), ("cs", B.v("cs"))],
                        inc=("ct" if j == 7 else None))
                if "halo" not in p4skip:
                    vop(lambda e, ec=ec: e.tensor_scalar(
                        smf[:, 64:67], PS0[:, 0:3], ipb_sb[:, ec:ec + 1], flag_c[:, 0:1],
                        op0=ALU.add, op1=ALU.mult), waits=[("ct", B.v("ct"))])
                    vop(lambda e, ec=ec: e.tensor_copy(xpreh[:, ec * 4:ec * 4 + 3], smf[:, 64:67]))
                halo_ev = B.v("cv")
                for tt in range(4):
                    q = PS0[:, tt * 512:(tt + 1) * 512]
                    for j in range(8):
                        top(lambda e, q=q, ipd=ipd, j=j, tt=tt: e.matmul(
                            q, ipd[:, j * 128:(j + 1) * 128],
                            hs_fm[:, :].rearrange("w (jj t) -> w jj t", jj=8)[:, j, tt * 512:(tt + 1) * 512],
                            start=(j == 0), stop=(j == 7)),
                            waits=[("cv", halo_ev)], inc=("ct" if j == 7 else None))
                    vop(lambda e, q=q, ec=ec, tt=tt: e.tensor_scalar(
                        xpre_s[:, tt * 512:(tt + 1) * 512], q, ipb_sb[:, ec:ec + 1], None,
                        op0=ALU.add), waits=[("ct", B.v("ct"))])
                xp_v = B.v("cv")
                if "conv" in p4skip:
                    vop(lambda e, ec=ec: e.tensor_copy(convT[:, :], xpre_s[:, :]))
                wk = lambda k: cw_sb[:, ec * 4 + k:ec * 4 + k + 1]
                if "conv" in p4skip:
                    pass
                if "conv" not in p4skip:
                    vop(lambda e, ec=ec: e.tensor_scalar(
                        convT[:, 3:2048], xpre_s[:, 0:2045], wk(0), None, op0=ALU.mult))
                    vop(lambda e, ec=ec: e.scalar_tensor_tensor(
                        convT[:, 3:2048], xpre_s[:, 1:2046], wk(1), convT[:, 3:2048],
                        op0=ALU.mult, op1=ALU.add))
                    vop(lambda e, ec=ec: e.scalar_tensor_tensor(
                        convT[:, 3:2048], xpre_s[:, 2:2047], wk(2), convT[:, 3:2048],
                        op0=ALU.mult, op1=ALU.add))
                    vop(lambda e, ec=ec: e.scalar_tensor_tensor(
                        convT[:, 3:2048], xpre_s[:, 3:2048], wk(3), convT[:, 3:2048],
                        op0=ALU.mult, op1=ALU.add))
                    vop(lambda e, ec=ec: e.tensor_copy(cat6[:, 0:3], xpreh[:, ec * 4:ec * 4 + 3]))
                    vop(lambda e: e.tensor_copy(cat6[:, 3:6], xpre_s[:, 0:3]))
                    vop(lambda e, ec=ec: e.tensor_scalar(
                        convT[:, 0:3], cat6[:, 0:3], wk(0), None, op0=ALU.mult))
                    vop(lambda e, ec=ec: e.scalar_tensor_tensor(
                        convT[:, 0:3], cat6[:, 1:4], wk(1), convT[:, 0:3], op0=ALU.mult, op1=ALU.add))
                    vop(lambda e, ec=ec: e.scalar_tensor_tensor(
                        convT[:, 0:3], cat6[:, 2:5], wk(2), convT[:, 0:3], op0=ALU.mult, op1=ALU.add))
                    vop(lambda e, ec=ec: e.scalar_tensor_tensor(
                        convT[:, 0:3], cat6[:, 3:6], wk(3), convT[:, 0:3], op0=ALU.mult, op1=ALU.add))
                cv_conv = B.v("cv")
                sop(lambda e, ec=ec: e.activation(xc_t(ec, 0, 2048), convT[:, :], AF.Silu,
                                                  bias=cb_sb[:, ec:ec + 1]),
                    waits=[("cv", cv_conv)])
                xc_done = B.v("cs")
                for tt in (range(4) if "xproj" not in p4skip else []):
                    top(lambda e, ec=ec, tt=tt: e.matmul(
                        PS1[0:96, tt * 512:(tt + 1) * 512],
                        xpw_sbv[:, ec * 96:(ec + 1) * 96],
                        xc_t(ec, tt * 512, (tt + 1) * 512),
                        start=(ec == 0), stop=(ec == 15)),
                        waits=[("cs", xc_done), ("dg", p4w_dg)],
                        inc=("ct" if (ec == 15 and tt == 3) else None))
            _ck(3.2)
            dbc_t = B.v("ct")
            vop(lambda e: e.tensor_copy(dtr_sbv, PS1[0:64, :]), waits=[("ct", dbc_t)])
            vop(lambda e: e.tensor_copy(bc_allv, PS1[64:96, :]))
            dbc_ev = B.v("cv")
            for tc in range(NTC):
                top(lambda e, tc=tc: e.transpose(
                    PS0b[:, 2048:2080], bc_allv[:, tc * 128:(tc + 1) * 128], bcperm_sb[:, :]),
                    waits=[("cv", dbc_ev if tc == 0 else B.v("cv")), ("dy", B.v("dy"))],
                    inc="ct")
                vop(lambda e: e.tensor_copy(bcst[:, :], PS0b[:, 2048:2080]),
                    waits=[("ct", B.v("ct"))])
                dma("sync", agbc_in[tc * 128:(tc + 1) * 128, :], bcst[:, :],
                    waits=[("cv", B.v("cv"))])
            _ck(3.4)
            for tc in range(NTC):
                for tt in range(4):
                    top(lambda e, tc=tc, tt=tt: e.matmul(
                        PS1[:, tt * 512:(tt + 1) * 512],
                        dtr_sbv[:, tc * 128:(tc + 1) * 128],
                        dpw_sb[:, tt * 512:(tt + 1) * 512],
                        start=True, stop=False),
                        waits=[("cs", B.v("cs")), ("dy", B.v("dy")), ("cv", dbc_ev),
                               ("dg", p4w_dg)])
                    top(lambda e, tc=tc, tt=tt: e.matmul(
                        PS1[:, tt * 512:(tt + 1) * 512],
                        ones1b[:, :],
                        dpb_sb[:, tt * 512:(tt + 1) * 512],
                        start=False, stop=True),
                        inc=("ct" if tt == 3 else None))
                dtp_t = B.v("ct")
                sop(lambda e: e.activation(sg1p4, PS1[:, :], AF.Sigmoid, scale=-1.0),
                    waits=[("ct", dtp_t)])
                sop(lambda e: e.activation(dtst[:, :], sg1p4, AF.Ln), inc="cs")
                dma("sync", ag_in[tc * 128:(tc + 1) * 128, 0:2048], dtst[:, :],
                    waits=[("cs", B.v("cs"))])
            _ck(3.6)
            for tc in range(NTC):
                for ec in range(16):
                    top(lambda e, tc=tc, ec=ec: e.transpose(
                        PS0b[:, ec * 128:(ec + 1) * 128],
                        xc_t(ec, tc * 128, (tc + 1) * 128), identb[:, :]),
                        waits=[("dy", B.v("dy"))], inc=("ct" if ec == 15 else None))
                vop(lambda e: e.tensor_copy(xctst[:, :], PS0b[:, 0:2048]),
                    waits=[("ct", B.v("ct"))])
                dma("sync", ag_in[tc * 128:(tc + 1) * 128, 2048:4096], xctst[:, :],
                    waits=[("cv", B.v("cv"))])
            p4_dy, p4_v, p4_s, p4_t = B.v("dy"), B.v("cv"), B.v("cs"), B.v("ct")
            _ck(4)

            gop(lambda e: e.collective_compute(
                "AllGather", ALU.bypass, replica_groups=[[0, 1], [2, 3], [4, 5], [6, 7]],
                ins=[ag_in.opt()], outs=[ag_out.opt()]),
                waits=[("dy", p4_dy)], inc="cc")
            _ck(4.5)
            gop(lambda e: e.collective_compute(
                "AllGather", ALU.bypass, replica_groups=[[0, 1], [2, 3], [4, 5], [6, 7]],
                ins=[agbc_in.opt()], outs=[agbc_out.opt()]),
                inc="cc")
            ag_done = B.v("cc")
            _ck(5)

            for eti in range(4):
                for sub in range(4):
                    dma("gpsimd", wg_sbv[:, sub * 1024:(sub + 1) * 1024],
                        ipw_r[16 + eti * 4 + sub], waits=[("ct", B.v("ct"))])
                wl = B.v("dg")
                for tc in range(NTC):
                    q = PS0[:, (tc % 2) * 512:(tc % 2) * 512 + 512]
                    k = 0
                    for j in range(8):
                        top(lambda e, q=q, j=j, tc=tc: e.matmul(
                            q, hs_fm[:, j * TOK + tc * 128:j * TOK + (tc + 1) * 128],
                            wg_sbv.rearrange("w (jj m) -> w jj m", jj=8)[:, j, :],
                            start=(j == 0), stop=False),
                            waits=[("dg", wl), ("cs", B.v("cs"))])
                    top(lambda e, q=q, eti=eti: e.matmul(
                        q, ones1b[:, :], ipbg_row[0:1, eti * 512:(eti + 1) * 512],
                        start=False, stop=True), inc="ct")
                    sop(lambda e, q=q: e.activation(f32t[:, 0:512], q, AF.Sigmoid),
                        waits=[("ct", B.v("ct"))])
                    sop(lambda e: e.copy(convT[:, 0:512], f32t[:, 0:512]),
                        waits=[("dy", B.v("dy"))], inc="cs")
                    dma("sync", sg_dram[tc * 128:(tc + 1) * 128, eti * 512:(eti + 1) * 512],
                        convT[:, 0:512], waits=[("cs", B.v("cs"))])
            p6_t, p6_dy = B.v("ct"), B.v("dy")
            _ck(6)

            for n in range(NL):
                dma("gpsimd", buf_fm[:, n * E:(n + 1) * E],
                    apos_dram[n, :].partition_broadcast(128),
                    waits=[("ct", p6_t), ("dc", apos_d)])
            abc_dg = B.v("dg")
            A_bc = buf_fm

            z_t = [ar_a[:, 0:2048], arn_c[:, 2112:4160]]
            e1_t = [ar_a[:, 2048:4096], ar_a[:, 4096:6144]]
            e2_t = [ar_a[:, 6144:8192], arn_c[:, 4160:6208]]
            v_t = [ar_a[:, 8192:10240], ar_a[:, 10240:12288]]
            eS_t = [ar_a[:, 12288:14336], ar_a[:, 14336:16384]]
            ndt_t = [ar_b[:, 0:2048], ar_b[:, 2048:4096]]
            xct_t = [ar_b[:, 4096:6144], ar_b[:, 6144:8192]]
            nDc_t = ar_b[:, 8192:10240]
            u_t = ar_b[:, 10240:12288]
            y_t = [ar_b[:, 12288:14336], ar_b[:, 14336:16384]]
            ybase = arn_c[:, 0:2048]
            bc32 = [arn_c[:, 2048:2080], arn_c[:, 2080:2112]]
            bcu = smf.ap()[:, 64:80]
            f32b = f32t.ap().bitcast(BF16)
            hrowp = [f32b[0:NL, 0:2048], f32b[0:NL, 2048:4096]]
            v0c = f32b[0:NL, 4096:6144]
            dcm_bc = xrb[0:NL, 22528:24576]
            z0c = xrb[0:NL, 24576:26624]
            e20c = xrb[0:NL, 26624:28672]
            dma("gpsimd", dhalf_bc, d_row[0, :].partition_broadcast(128),
                waits=[("dy", spill_dy), ("ct", B.v("ct")), ("cv", B.v("cv"))])
            vop(lambda e: e.tensor_scalar(dhalf_bc, dhalf_bc, 0.5, None, op0=ALU.mult),
                waits=[("dg", B.v("dg"))])
            dma("sync", aposv, apos_dram[:, :], waits=[("dc", apos_d), ("dy", B.v("dy"))])
            aposv_dy = B.v("dy")

            vop(lambda e: e.memset(ndt_t[0][:, :], 0.0), waits=[("ct", p4_t), ("dy", p6_dy),
                                                                ("cs", B.v("cs"))])
            vop(lambda e: e.memset(hrowp[0], 0.0))
            vop(lambda e: e.memset(hrowp[1], 0.0))
            vop(lambda e: e.memset(ndt_t[1][:, :], 0.0))
            vop(lambda e: e.memset(xct_t[0][:, :], 0.0))
            vop(lambda e: e.memset(xct_t[1][:, :], 0.0))
            vop(lambda e: e.memset(bc32[0][:, :], 0.0))
            vop(lambda e: e.memset(bc32[1][:, :], 0.0))
            zinit_v = B.v("cv")

            chunk_end = []
            csm, ctm, dcmk = [], [], []
            for c in range(NCH):
                p = c % 2
                ns = min(TC, SEQ - c * TC)
                r0 = c * TC
                if ns < TC:
                    vop(lambda e, p=p: e.memset(ndt_t[p][:, :], 0.0),
                        waits=[("ct", B.v("ct")), ("cv", B.v("cv"))])
                    vop(lambda e, p=p: e.memset(xct_t[p][:, :], 0.0))
                    vop(lambda e, p=p: e.memset(bc32[p][:, :], 0.0))
                wstr = [("cv", zinit_v), ("cc", ag_done)]
                if c >= 2:
                    wstr = [("cv", chunk_end[c - 2]["cv"]), ("ct", chunk_end[c - 2]["ct"]),
                            ("cc", ag_done)]
                if ns < TC:
                    wstr = [("cv", B.v("cv")), ("cc", ag_done)]
                dma("sync", ndt_t[p][1:1 + ns, :], ag_out[r0:r0 + ns, 0:2048], waits=wstr)
                dma("sync", xct_t[p][1:1 + ns, :], ag_out[r0:r0 + ns, 2048:4096], waits=wstr)
                dma("sync", bc32[p][1:1 + ns, :], agbc_out[r0:r0 + ns, :], waits=wstr)
                ld = B.v("dy")
                vop(lambda e, p=p: e.tensor_scalar(
                    bcu[:, 0:8], bc32[p][:, 0:8], ab_col[:, 0:1], None, op0=ALU.mult),
                    waits=[("dy", ld)])
                vop(lambda e, p=p: e.scalar_tensor_tensor(
                    bcu[:, 0:8], bc32[p][:, 8:16], ab_col[:, 1:2], bcu[:, 0:8],
                    op0=ALU.mult, op1=ALU.add))
                vop(lambda e, p=p: e.tensor_scalar(
                    bcu[:, 8:16], bc32[p][:, 16:24], ab_col[:, 0:1], None, op0=ALU.mult))
                vop(lambda e, p=p: e.scalar_tensor_tensor(
                    bcu[:, 8:16], bc32[p][:, 24:32], ab_col[:, 1:2], bcu[:, 8:16],
                    op0=ALU.mult, op1=ALU.add))
                vop(lambda e, p=p: e.tensor_tensor(u_t[:, :], ndt_t[p][:, :], xct_t[p][:, :],
                                                   op=ALU.mult))
                vop(lambda e, p=p: e.tensor_tensor(ybase[:, :], xct_t[p][:, :], dhalf_bc[:, :],
                                                   op=ALU.mult))
                pre_v = B.v("cv")
                for tt in range(4):
                    top(lambda e, p=p, tt=tt: e.matmul(
                        PS0[:, tt * 512:(tt + 1) * 512], tric[:, :],
                        ndt_t[p][:, tt * 512:(tt + 1) * 512], start=True, stop=True),
                        waits=[("dy", ld), ("cv", B.v("cv"))],
                        inc=("ct" if tt == 3 else None))
                sop(lambda e: e.copy(nDc_t[:, :], PS0[:, :]),
                    waits=[("ct", B.v("ct")), ("cv", B.v("cv"))], inc="cs")
                ndc_s = B.v("cs")
                if c > 0:
                    dma("scalar", dcm_dram[:, :], nDc_t[0:1, :], waits=[("cs", ndc_s)])
                    dma("scalar", dcm_bc[:, :], dcm_dram[0, :].partition_broadcast(NL),
                        waits=[("dc", B.v("dc"))])
                    vop(lambda e: e.tensor_tensor(z0c, aposv, dcm_bc, op=ALU.mult),
                        waits=[("dc", B.v("dc")), ("dy", aposv_dy)])
                    sop(lambda e: e.activation(e20c, z0c, AF.Exp, scale=-1.0),
                        waits=[("cv", B.v("cv"))])
                    vop(lambda e, c=c: e.tensor_tensor(
                        v0c, e20c, hrowp[(c - 1) % 2],
                        op=ALU.mult), waits=[("cs", B.v("cs")), ("dc", B.v("dc"))])
                    v0_v = B.v("cv")
                stz = {}; st1 = {}
                for k in range(NL + 2):
                    if k < NL:
                        n, np_ = k, k % 2
                        zw = [("cs", ndc_s), ("dg", abc_dg)]
                        if len(csm) >= 2:
                            zw.append(("cs", csm[-2]))
                        vop(lambda e, n=n, np_=np_: e.tensor_tensor(
                            z_t[np_][:, :], nDc_t[:, :], A_bc[:, n * E:(n + 1) * E],
                            op=ALU.mult), waits=zw)
                        stz[n] = B.v("cv")
                    if 1 <= k <= NL:
                        n, np_ = k - 1, (k - 1) % 2
                        sop(lambda e, np_=np_: e.activation(
                            e1_t[np_][:, :], z_t[np_][:, :], AF.Exp),
                            waits=[("cv", B.v("cv"))])
                        sop(lambda e, np_=np_: e.activation(
                            e2_t[np_][:, :], z_t[np_][:, :], AF.Exp, scale=-1.0))
                        e12_s = B.v("cs")
                        csm.append(e12_s)
                        vw = [("cs", e12_s)]
                        if len(ctm) >= 2:
                            vw.append(("ct", ctm[-2]))
                        vop(lambda e, np_=np_, n=n: e.scalar_tensor_tensor(
                            v_t[np_][:, :], u_t[:, :], bcu[:, n:n + 1], e2_t[np_][:, :],
                            op0=ALU.mult, op1=ALU.mult), waits=vw)
                        vv = B.v("cv")
                        if c > 0:
                            dma("scalar", v_t[np_][0:1, :], v0c[n:n + 1, :],
                                waits=[("cv", vv)])
                        vsc = B.v("dc")
                        PSn = PS1 if np_ == 0 else PS0
                        for tt in range(4):
                            top(lambda e, PSn=PSn, np_=np_, tt=tt: e.matmul(
                                PSn[:, tt * 512:(tt + 1) * 512], trip[:, :],
                                v_t[np_][:, tt * 512:(tt + 1) * 512],
                                start=True, stop=True),
                                waits=[("cv", vv)] + ([("dc", vsc)] if c > 0 else []),
                                inc=("ct" if tt == 3 else None))
                        smm = B.v("ct")
                        ctm.append(smm)
                        st1[n] = (np_, PSn, smm, e12_s)
                    if k >= 2:
                        n = k - 2
                        np_, PSn, smm, e12_s = st1[n]
                        ew = [("ct", smm), ("cs", e12_s)]
                        if len(dcmk) >= 2:
                            ew.append(("dc", dcmk[-2]))
                        vop(lambda e, PSn=PSn, np_=np_: e.tensor_tensor(
                            eS_t[np_][:, :], e1_t[np_][:, :], PSn[:, :], op=ALU.mult),
                            waits=ew)
                        esv = B.v("cv")
                        dma("scalar", hrowp[c % 2][n:n + 1, :],
                            eS_t[np_][ns:ns + 1, :], waits=[("cv", esv)])
                        dcmk.append(B.v("dc"))
                        prev = ybase if n == 0 else y_t[(n - 1) % 2]
                        vop(lambda e, np_=np_, n=n, prev=prev: e.scalar_tensor_tensor(
                            y_t[np_][:, :], eS_t[np_][:, :], bcu[:, 8 + n:9 + n], prev,
                            op0=ALU.mult, op1=ALU.add),
                            waits=[("cv", pre_v), ("dy", ld)])
                yv = B.v("cv")
                dma("sync", y_in[r0:r0 + ns, :], y_t[(NL - 1) % 2][1:1 + ns, :],
                    waits=[("cv", yv)])
                chunk_end.append({"cv": B.v("cv"), "ct": B.v("ct"), "dy": B.v("dy"),
                                  "dc": B.v("dc"), "cs": B.v("cs")})
            p7 = chunk_end[-1]
            _ck(7)

            gop(lambda e: e.collective_compute(
                "ReduceScatter", ALU.add, replica_groups=[[0, 1], [2, 3], [4, 5], [6, 7]],
                ins=[y_in.opt()], outs=[y_out.opt()]),
                waits=[("dy", p7["dy"])], inc="cc")
            rs_done = B.v("cc")
            for ec in range(16):
                dma("gpsimd", buf_fm[:, ec * 1024:(ec + 1) * 1024],
                    opw_d[ec * 128:(ec + 1) * 128, :],
                    waits=[("cv", p7["cv"])])
            opw_dg = B.v("dg")
            opw_sb = buf_fm
            _ck(8)

            for tc in range(NTC):
                p = tc % 2
                sgt = ar_a[:, p * 2048:p * 2048 + 2048]
                yft = ar_a[:, 4096 + p * 2048:4096 + p * 2048 + 2048]
                ytot = ar_a[:, 8192 + p * 2048:8192 + p * 2048 + 2048]
                yEst = arn_c[:, 4096 + p * 2048:4096 + p * 2048 + 2048]
                war = [("cv", B.v("cv")), ("ct", B.v("ct"))] if tc >= 2 else \
                      [("cv", p7["cv"]), ("ct", p7["ct"])]
                dma("sync", sgt, sg_dram[tc * 128:(tc + 1) * 128, :], waits=war + [("dy", p6_dy)])
                dma("sync", yft, y_out[tc * 128:(tc + 1) * 128, :], waits=war + [("cc", rs_done)])
                ld = B.v("dy")
                vop(lambda e, sgt=sgt, yft=yft, ytot=ytot: e.tensor_tensor(
                    ytot, sgt, yft, op=ALU.mult), waits=[("dy", ld)])
                ytv = B.v("cv")
                for ec in range(16):
                    top(lambda e, ytot=ytot, ec=ec: e.transpose(
                        PS0b[:, ec * 128:(ec + 1) * 128],
                        ytot[:, ec * 128:(ec + 1) * 128], identb[:, :]),
                        waits=[("cv", ytv)], inc=("ct" if ec == 15 else None))
                vop(lambda e, yEst=yEst: e.tensor_copy(yEst, PS0b[:, 0:2048]),
                    waits=[("ct", B.v("ct"))])
                yev = B.v("cv")
                for hh in range(2):
                    q = PS1[:, (tc % 2) * 1024 + hh * 512:(tc % 2) * 1024 + (hh + 1) * 512]
                    for ec in range(16):
                        top(lambda e, q=q, yEst=yEst, ec=ec, hh=hh: e.matmul(
                            q, yEst[:, ec * 128:(ec + 1) * 128],
                            opw_sb[:, ec * 1024 + hh * 512:ec * 1024 + (hh + 1) * 512],
                            start=(ec == 0), stop=(ec == 15)),
                            waits=[("cv", yev), ("dg", opw_dg)],
                            inc=("ct" if ec == 15 else None))
                dma("sync", f32t[:, 2 * H:3 * H], xres_dram[tc * 128:(tc + 1) * 128, :],
                    waits=[("cv", B.v("cv"))])
                xin_dy = B.v("dy")
                opt = B.v("ct")
                vop(lambda e, tc=tc, p=p: e.tensor_tensor(
                    f32t[:, p * H:(p + 1) * H], PS1[:, (tc % 2) * 1024:(tc % 2) * 1024 + 1024],
                    f32t[:, 2 * H:3 * H], op=ALU.add),
                    waits=[("ct", opt), ("dy", xin_dy)])
                dma("sync", out_loc[tc * 128:(tc + 1) * 128, :], f32t[:, p * H:(p + 1) * H],
                    waits=[("cv", B.v("cv"))])
            B.emit("sync", lambda e: e.nop(), [("dy", B.v("dy"))], None)


        except _StopBuild:
            partial[0] = True
        if debug:
            for nm, src in ([("sb_arb", ar_b.ap()), ("sb_arnc", arn_c.ap()),
                             ("sb_buffm", buf_fm.ap()), ("sb_gwhi", gw_hi.ap()),
                             ("sb_gwlo", gw_lo.ap()), ("sb_gwf", gw_f.ap())]
                            if upto <= 2 else []) + [
                            ("xres_dram", xres_dram), ("ag_out", ag_out),
                            ("agbc_out", agbc_out), ("y_in", y_in), ("y_out", y_out),
                            ("sg_dram", sg_dram), ("swt_dram", swt_dram),
                            ("wm_dram", wm_dram), ("r_out", r_out),
                            ("apos_dram", apos_dram), ("ag_in", ag_in), ("r_in", r_in),
                            ("agbc_in", agbc_in)]:
                try:
                    d = nc.dram_tensor("dbg_" + nm, list(src.shape), src.dtype,
                                       kind="ExternalOutput").ap()
                    allw = [(sn, B.v(sn)) for sn in
                            ("dy", "dc", "dg", "cv", "cs", "ct", "cg", "cc") if B.v(sn) > 0]
                    dma("sync", d, src, waits=allw)
                except Exception:
                    pass
            B.emit("sync", lambda e: e.nop(), [("dy", B.v("dy"))], None)
        if partial[0]:
            allw = [(sn, B.v(sn)) for sn in ("dy", "dc", "dg", "cv", "cs", "ct", "cg", "cc") if B.v(sn) > 0]
            for tc in range(NTC):
                dma("sync", out_loc[tc * 128:(tc + 1) * 128, :], f32t[:, 0:H], waits=allw)
            B.emit("sync", lambda e: e.nop(), [("dy", B.v("dy"))], None)

        COMPUTE_SEMS = {"cv", "cs", "ct", "cg"}
        waited = defaultdict(set)
        for eng in ENGS:
            for fn, waits, inc, amt in B.ops[eng]:
                for sname, vv in waits:
                    waited[sname].add(vv)
        for eng in ENGS:
            cum = defaultdict(int)
            pend = defaultdict(int)
            newops = []
            for fn, waits, inc, amt in B.ops[eng]:
                inc2, amt2 = inc, amt
                if inc is not None and inc in COMPUTE_SEMS:
                    cum[inc] += amt
                    pend[inc] += amt
                    if cum[inc] in waited[inc]:
                        inc2, amt2 = inc, pend[inc]
                        pend[inc] = 0
                    else:
                        inc2 = None
                newops.append((fn, waits, inc2, amt2))
            B.ops[eng] = newops

        def run_stream(eng_handle, ops):
            hw = defaultdict(int)
            for fn, waits, inc, amt in ops:
                for s, vv in waits:
                    if vv > hw[s]:
                        eng_handle.wait_ge(sems[s], vv)
                        hw[s] = vv
                instr = fn(eng_handle)
                if inc is not None:
                    instr.then_inc(sems[inc], amt, skip_validation=True)

        @blk.sync
        def _(eng):
            run_stream(eng, B.ops["sync"])

        @blk.scalar
        def _(eng):
            run_stream(eng, B.ops["scalar"])

        @blk.vector
        def _(eng):
            run_stream(eng, B.ops["vector"])

        @blk.tensor
        def _(eng):
            run_stream(eng, B.ops["tensor"])

        @blk.gpsimd
        def _(eng):
            run_stream(eng, B.ops["gpsimd"])

    return nc


_NC_CACHE = None


def _get_nc():
    global _NC_CACHE
    if _NC_CACHE is None:
        _NC_CACHE = build_nc()
    return _NC_CACHE


def kernel(**inputs):
    f32 = lambda k: np.ascontiguousarray(np.asarray(inputs[k], dtype=np.float32))
    x = f32("L0_attn_hidden_states")
    ipw_h = f32("L0_attn_in_proj_w")
    ipw_r_h = np.ascontiguousarray(
        ipw_h.reshape(8, 128, 32, 128).transpose(2, 1, 0, 3).reshape(32, 128, 1024))
    ipb_full = f32("L0_attn_in_proj_b")
    ipb_col_h = np.ascontiguousarray(ipb_full[0:E].reshape(16, 128).T)
    cw_full = f32("L0_attn_conv_w")
    cw_col_h = np.ascontiguousarray(cw_full.reshape(16, 128, 4).transpose(1, 0, 2).reshape(128, 64))
    cb_col_h = np.ascontiguousarray(f32("L0_attn_conv_b").reshape(16, 128).T)
    xpw_h = f32("L0_attn_x_proj_w")
    xpw_r_h = np.ascontiguousarray(
        xpw_h.reshape(16, 128, 96).transpose(1, 0, 2).reshape(128, 16 * 96))
    dpw_h = f32("L0_attn_dt_proj_w")
    dpb_h = f32("L0_attn_dt_proj_b").reshape(1, E)
    alog_t = np.ascontiguousarray(f32("L0_attn_A_log").T)
    d_h = f32("L0_attn_D").reshape(1, E)
    opw_h = f32("L0_attn_out_proj_w")
    opb_h = f32("L0_attn_out_proj_b").reshape(1, H)
    lnw_h = f32("L0_attn_ln_w").reshape(1, H)
    lnb_h = f32("L0_attn_ln_b").reshape(1, H)
    gwt_h = np.ascontiguousarray(f32("L0_ffn_gate_w").T)
    gwt_r_h = np.ascontiguousarray(gwt_h.reshape(8, 128, 8).transpose(1, 0, 2).reshape(128, 64))
    lb_h = f32("L0_ffn_lb_bias").reshape(8, 1)
    def _exp_gu(w):
        wt = w.transpose(0, 2, 1)
        return np.ascontiguousarray(
            wt.reshape(8, 8, 128, 4, 128).transpose(0, 2, 3, 1, 4).reshape(8, 128, 4096))

    egwt_r_h = _exp_gu(f32("L0_ffn_expert_gate_w"))
    euwt_r_h = _exp_gu(f32("L0_ffn_expert_up_w"))
    edwt_t = f32("L0_ffn_expert_down_w").transpose(0, 2, 1)
    edwt_r_h = np.ascontiguousarray(
        edwt_t.reshape(8, 4, 128, 1024).transpose(0, 2, 1, 3).reshape(8, 128, 4096))

    def _sh_gu(w):
        wt = np.ascontiguousarray(w.T)
        return np.ascontiguousarray(
            wt.reshape(8, 128, 2, 4, 128).transpose(1, 2, 3, 0, 4).reshape(128, 8192))

    sgwt_r_h = _sh_gu(f32("L0_ffn_shared_gate_w"))
    suwt_r_h = _sh_gu(f32("L0_ffn_shared_up_w"))
    sdwt_t = np.ascontiguousarray(f32("L0_ffn_shared_down_w").T)
    sdwt_r_h = np.ascontiguousarray(
        sdwt_t.reshape(2, 4, 128, 1024).transpose(2, 0, 1, 3).reshape(128, 8192))
    bcperm_h = np.eye(32, dtype=np.float32)

    in_maps = []
    for c in range(8):
        b, h = c // 2, c % 2
        xl = np.ascontiguousarray(x[b, h * TOK:(h + 1) * TOK, :])
        if h == 1:
            xh = np.ascontiguousarray(x[b, TOK - 3:TOK, :])
            hf = np.ones((1, 1), np.float32)
        else:
            xh = np.zeros((3, H), np.float32)
            hf = np.zeros((1, 1), np.float32)
        alog_loc = np.ascontiguousarray(alog_t[h * NL:(h + 1) * NL, :])
        ab = np.array([[1.0 - h, float(h), -(1.0 - h), -float(h)]], np.float32)
        in_maps.append({
            "x_loc": xl, "x_halo": xh, "haloflag": hf,
            "ln_w": lnw_h, "ln_b": lnb_h,
            "ipw_r": ipw_r_h, "ipb_col": ipb_col_h, "ipb_row": ipb_full.reshape(1, E2),
            "cw_col": cw_col_h, "cb_col": cb_col_h, "xpw_r": xpw_r_h,
            "dpw": dpw_h, "dpb_row": dpb_h,
            "a_logt": alog_loc, "d_row": d_h,
            "opw": opw_h, "opb_row": opb_h,
            "gwt_r": gwt_r_h, "lb": lb_h,
            "egwt_r": egwt_r_h, "euwt_r": euwt_r_h, "edwt_r": edwt_r_h,
            "sgwt_r": sgwt_r_h, "suwt_r": suwt_r_h, "sdwt_r": sdwt_r_h,
            "bcperm": bcperm_h, "alphab": ab,
        })
    nc = _get_nc()
    res = run_bass_kernel_spmd(nc, in_maps, core_ids=list(range(8)))
    out = np.empty((4, SEQ, H), np.float32)
    for c in range(8):
        b, h = c // 2, c % 2
        out[b, h * TOK:(h + 1) * TOK, :] = res.results[c]["out_loc"]
    return out


# version 14
# speedup vs baseline: 1.4910x; 1.4910x over previous
import numpy as np
from collections import defaultdict
from contextlib import ExitStack

import concourse.bass as bass
import concourse.mybir as mybir
from concourse.bass_utils import run_bass_kernel_spmd

F32 = mybir.dt.float32
BF16 = mybir.dt.bfloat16
I32 = mybir.dt.int32
AF = mybir.ActivationFunctionType
ALU = mybir.AluOpType
AXX = mybir.AxisListType.X

H = 1024
E = 2048
E2 = 4096
NL = 8
RK = 64
TOK = 2048
NTC = 16
SEQ = 4096
TC = 127
NCH = 33
MIDS = 64
EPS = 1e-5

ENGS = ("sync", "scalar", "vector", "tensor", "gpsimd")
QSEM = {"sync": "dy", "scalar": "dc", "gpsimd": "dg"}


class Build:
    def __init__(self):
        self.ops = {e: [] for e in ENGS}
        self.tot = defaultdict(int)

    def emit(self, eng, fn, waits=(), inc=None, amt=1):
        self.ops[eng].append((fn, tuple(waits), inc, amt))
        if inc is not None:
            self.tot[inc] += amt

    def v(self, sem):
        return self.tot[sem]


class _StopBuild(Exception):
    pass


def build_nc(upto=99, skip_ffn=False, detect_races=False, p4skip=(), debug=False):
    nc = bass.Bass("TRN2", target_bir_lowering=False, dynamic_dma_scratch_size=8192,
                   detect_race_conditions=detect_races)
    B = Build()

    def din(name, shape, dt=F32):
        return nc.dram_tensor(name, shape, dt, kind="ExternalInput").ap()

    x_loc = din("x_loc", [TOK, H])
    x_halo = din("x_halo", [3, H])
    haloflag = din("haloflag", [1, 1])
    ln_w = din("ln_w", [1, H]); ln_b = din("ln_b", [1, H])
    ipw_r = din("ipw_r", [32, 128, 1024])
    ipb_col = din("ipb_col", [128, 16]); ipb_row = din("ipb_row", [1, E2])
    cw_col = din("cw_col", [128, 64]); cb_col = din("cb_col", [128, 16])
    xpw_r = din("xpw_r", [128, 16 * 96])
    dpw = din("dpw", [RK, E]); dpb_row = din("dpb_row", [1, E])
    a_logt = din("a_logt", [NL, E])
    d_row = din("d_row", [1, E])
    opw_d = din("opw", [E, H]); opb_row = din("opb_row", [1, H])
    gwt_r = din("gwt_r", [128, 64]); lb = din("lb", [8, 1])
    egwt_r = din("egwt_r", [8, 128, 4096]); euwt_r = din("euwt_r", [8, 128, 4096])
    edwt_r = din("edwt_r", [8, 128, 4096])
    sgwt_r = din("sgwt_r", [128, 8192]); suwt_r = din("suwt_r", [128, 8192])
    sdwt_r = din("sdwt_r", [128, 8192])
    bcperm = din("bcperm", [32, 32])
    alphab = din("alphab", [1, 4])

    out_loc = nc.dram_tensor("out_loc", [TOK, H], F32, kind="ExternalOutput").ap()
    dbg = {}
    dbg_sc = nc.dram_tensor("dbg_sc", [8, TOK], F32, kind="ExternalOutput").ap()
    dbg_sct = nc.dram_tensor("dbg_sct", [128, NTC * 8], F32, kind="ExternalOutput").ap()
    dbg_wts = nc.dram_tensor("dbg_wts", [128, NTC * 16], F32, kind="ExternalOutput").ap()

    ag_in = nc.dram_tensor("ag_in", [TOK, E2], BF16, kind="Internal").ap()
    ag_out = nc.dram_tensor("ag_out", [SEQ, E2], BF16, kind="Internal").ap()
    agbc_in = nc.dram_tensor("agbc_in", [TOK, 32], BF16, kind="Internal").ap()
    agbc_out = nc.dram_tensor("agbc_out", [SEQ, 32], BF16, kind="Internal").ap()
    y_inQ = [nc.dram_tensor(f"y_inQ{q}", [1024, E], BF16, kind="Internal").ap()
             for q in range(4)]
    y_outQ = [nc.dram_tensor(f"y_outQ{q}", [512, E], BF16, kind="Internal").ap()
              for q in range(4)]
    r_in = nc.dram_tensor("r_in", [1, 16], F32, kind="Internal").ap()
    r_out = nc.dram_tensor("r_out", [1, 16], F32, kind="Internal").ap()
    sg_dram = nc.dram_tensor("sg_dram", [TOK, E], BF16, kind="Internal").ap()
    dcm_dram = nc.dram_tensor("dcm_dram", [1, E], BF16, kind="Internal").ap()
    wm_dram = nc.dram_tensor("wm_dram", [1, 8], F32, kind="Internal").ap()
    swt_dram = nc.dram_tensor("swt_dram", [8, TOK], BF16, kind="Internal").ap()
    apos_dram = nc.dram_tensor("apos_dram", [NL, E], BF16, kind="Internal").ap()
    xres_dram = nc.dram_tensor("xres_dram", [TOK, H], F32, kind="Internal").ap()

    with ExitStack() as ctx:
        SB = lambda name, shape, dt: ctx.enter_context(nc.sbuf_tensor(name, shape, dt))
        xres = SB("xres", [128, NTC * H], F32)
        buf_fm = SB("buf_fm", [128, 8 * TOK], BF16)
        arn_c = SB("arn_c", [128, 8 * E], BF16)
        ar_a = SB("ar_a", [128, 8 * E], BF16)
        ar_b = SB("ar_b", [128, 8 * E], BF16)
        araf = ar_a.ap().bitcast(F32)
        xrb = xres.ap().bitcast(BF16)
        lnw_bc = araf[:, 2048:3072]
        lnb_bc = araf[:, 3072:4096]
        opb_bc = araf[:, 4096:5120]
        iota_p = SB("iota_p", [128, 128], I32)
        iota_f = SB("iota_f", [128, 128], I32)
        tric = SB("tric", [128, 128], BF16)
        trip = SB("trip", [128, 128], BF16)
        identb = SB("identb", [128, 128], BF16)
        trif32 = SB("trif32", [128, 128], F32)
        ones128f = SB("ones128f", [128, 1], F32)
        ones1b = SB("ones1b", [1, 128], BF16)
        ones8 = SB("ones8", [128, 8], F32)
        ipb_sb = SB("ipb_sb", [128, 16], F32)
        cw_sb = SB("cw_sb", [128, 64], F32)
        cb_sb = SB("cb_sb", [128, 16], F32)
        dpw_sb = xrb[0:RK, 0:2048]
        dtr_sbv = xrb[0:RK, 2048:4096]
        bc_allv = xrb[0:32, 4096:6144]
        ip_sbv = xrb[:, 6144:8192]
        wg_sbv = xrb[:, 8192:12288]
        dpb_sb = xrb[0:1, 12288:14336]
        ipbg_row = xrb[0:1, 14336:16384]
        xpw_sbv = xrb[:, 16384:17920]
        dhalf_bc = xrb[:, 17920:19968]
        aposv = xrb[0:NL, 19968:22016]
        bcperm_sb = SB("bcperm_sb", [32, 32], BF16)
        ab_col = SB("ab_col", [128, 4], F32)
        gw_hi = SB("gw_hi", [128, 64], BF16)
        gw_lo = SB("gw_lo", [128, 64], BF16)
        gw_f = SB("gw_f", [128, 64], F32)
        lb_sb = SB("lb_sb", [8, 1], F32)
        scT = SB("scT", [128, NTC * 8], F32)
        smf = SB("smf", [128, 128], F32)
        wtsel = SB("wtsel", [128, NTC * 16], F32)
        wm_bc = SB("wm_bc", [128, 8], F32)
        selw8 = SB("selw8", [128, 8], BF16)
        swt_st = SB("swt_st", [8, 128], BF16)
        xch = SB("xch", [128, H], F32)
        f32t = SB("f32t", [128, 3 * H], F32)
        stat = SB("stat", [128, 16], F32)
        halo3 = araf[:, 5120:6144][0:3, :]
        halo_fm = SB("halo_fm", [128, 32], BF16)
        flag_c = SB("flag_c", [128, 1], F32)
        xpreh = SB("xpreh", [128, 64], BF16)
        rres = SB("rres", [1, 16], F32)
        bcst = SB("bcst", [128, 32], BF16)
        scoresS = f32t.ap()[0:8, 0:E]
        sg1p4 = xres.ap()[:, 11264:13312]
        PS0 = ctx.enter_context(nc.psum_tensor("PS0", [128, 2048], F32))
        PS1 = ctx.enter_context(nc.psum_tensor("PS1", [128, 2048], F32))

        sems = {}
        for s in ("dg", "dy", "dc", "cv", "cs", "ct", "cg", "cc"):
            sems[s] = ctx.enter_context(nc.semaphore("sem_" + s))

        blk = ctx.enter_context(nc.Block())

        def _ck(k):
            if upto <= k:
                raise _StopBuild

        PS0b = PS0.ap().bitcast(BF16)
        PS1b = PS1.ap().bitcast(BF16)

        def dma(q, out, in_, waits=()):
            B.emit(q, lambda e, o=out, i=in_: e.dma_start(out=o, in_=i), waits, QSEM[q], 16)

        def vop(fn, waits=(), inc="cv", drain=False):
            B.emit("vector", fn, waits, inc)
            if drain:
                B.emit("vector", lambda e: e.drain(), (), None)

        def sop(fn, waits=(), inc="cs"):
            B.emit("scalar", fn, waits, inc)

        def top(fn, waits=(), inc=None):
            B.emit("tensor", fn, waits, inc)

        def gop(fn, waits=(), inc="cg"):
            B.emit("gpsimd", fn, waits, inc)

        partial = [False]
        try:
            pass
        except Exception:
            pass
        try:
            dma("sync", lnw_bc, ln_w[0, :].partition_broadcast(128))
            dma("sync", lnb_bc, ln_b[0, :].partition_broadcast(128))
            dma("sync", opb_bc, opb_row[0, :].partition_broadcast(128))
            dma("sync", flag_c[:, :], haloflag[0, 0:1].partition_broadcast(128))
            dma("sync", ab_col[:, :], alphab[0, :].partition_broadcast(128))
            dma("sync", halo3, x_halo[:, :])
            dma("sync", ipb_sb[:, :], ipb_col[:, :])
            dma("sync", cw_sb[:, :], cw_col[:, :])
            dma("sync", cb_sb[:, :], cb_col[:, :])
            dma("sync", f32t[0:NL, 0:E], a_logt[:, :])
            dma("gpsimd", bcperm_sb[:, :], bcperm[:, :])
            dma("gpsimd", gw_f[:, :], gwt_r[:, :])
            dma("gpsimd", lb_sb[:, :], lb[:, :])
            w0g, w0y = B.v("dg"), B.v("dy")
            gop(lambda e: e.iota(iota_p[:, :], pattern=[[0, 128]], base=0, channel_multiplier=1))
            gop(lambda e: e.iota(iota_f[:, :], pattern=[[1, 128]], base=0, channel_multiplier=0))
            iota_g = B.v("cg")
            vop(lambda e: e.tensor_tensor(trif32[:, :], iota_p[:, :], iota_f[:, :], op=ALU.is_le),
                waits=[("cg", iota_g)])
            vop(lambda e: e.tensor_scalar(tric[:, :], iota_p[:, :], float(MIDS), None, op0=ALU.is_le))
            vop(lambda e: e.tensor_tensor(tric[:, :], trif32[:, :], tric[:, :], op=ALU.subtract))
            vop(lambda e: e.tensor_copy(trip[:, :], trif32[:, :]))
            vop(lambda e: e.tensor_tensor(identb[:, :], iota_p[:, :], iota_f[:, :], op=ALU.is_equal))
            vop(lambda e: e.memset(ones128f[:, :], 1.0))
            vop(lambda e: e.memset(ones1b[:, :], 1.0))
            vop(lambda e: e.memset(ones8[:, :], 1.0))
            sop(lambda e: e.activation(ar_b[0:NL, 0:E], f32t[0:NL, 0:E], AF.Exp),
                waits=[("dy", w0y)])
            apos_s = B.v("cs")
            dma("scalar", apos_dram[:, :], ar_b[0:NL, 0:E], waits=[("cs", apos_s)])
            apos_d = B.v("dc")
            vop(lambda e: e.tensor_copy(gw_hi[:, :], gw_f[:, :]), waits=[("dg", w0g)])
            vop(lambda e: e.tensor_tensor(gw_lo[:, :], gw_f[:, :], gw_hi[:, :], op=ALU.subtract))
            p0_v, p0_s = B.v("cv"), B.v("cs")
            _ck(0)

            def emit_ln(src, nrow, xn_dst, hs_dst, w=()):
                s0 = stat[0:nrow, 0:1]; s1 = stat[0:nrow, 1:2]; s2 = stat[0:nrow, 2:3]
                sq = f32t[0:nrow, 2 * H:3 * H]
                sop(lambda e: e.activation(sq, src, AF.Square), waits=w)
                vop(lambda e: e.reduce_sum(s0, src, axis=AXX), waits=w, drain=True)
                vop(lambda e: e.reduce_sum(s1, sq, axis=AXX), waits=[("cs", B.v("cs"))])
                vop(lambda e: e.tensor_scalar(s0, s0, 1.0 / H, None, op0=ALU.mult))
                vop(lambda e: e.tensor_scalar(s1, s1, 1.0 / H, None, op0=ALU.mult))
                vop(lambda e: e.tensor_tensor(s2, s0, s0, op=ALU.mult))
                vop(lambda e: e.tensor_tensor(s1, s1, s2, op=ALU.subtract))
                vop(lambda e: e.tensor_scalar(s1, s1, EPS, None, op0=ALU.add))
                sop(lambda e: e.activation(s2, s1, AF.Sqrt), waits=[("cv", B.v("cv"))])
                vop(lambda e: e.reciprocal(s2, s2), waits=[("cs", B.v("cs"))])
                vop(lambda e: e.tensor_scalar(xn_dst, src, s0, s2, op0=ALU.subtract, op1=ALU.mult))
                sop(lambda e: e.activation(sq, xn_dst, AF.Square), waits=[("cv", B.v("cv"))])
                vop(lambda e: e.reduce_sum(s0, xn_dst, axis=AXX))
                vop(lambda e: e.reduce_sum(s1, sq, axis=AXX), waits=[("cs", B.v("cs"))])
                vop(lambda e: e.tensor_scalar(s0, s0, 1.0 / H, None, op0=ALU.mult))
                vop(lambda e: e.tensor_scalar(s1, s1, 1.0 / H, None, op0=ALU.mult))
                vop(lambda e: e.tensor_tensor(s2, s0, s0, op=ALU.mult))
                vop(lambda e: e.tensor_tensor(s1, s1, s2, op=ALU.subtract))
                vop(lambda e: e.tensor_scalar(s1, s1, EPS, None, op0=ALU.add))
                sop(lambda e: e.activation(s2, s1, AF.Sqrt), waits=[("cv", B.v("cv"))])
                vop(lambda e: e.reciprocal(s2, s2), waits=[("cs", B.v("cs"))])
                vop(lambda e: e.tensor_scalar(hs_dst, xn_dst, s0, s2, op0=ALU.subtract, op1=ALU.mult))
                vop(lambda e: e.tensor_tensor(hs_dst, hs_dst, lnw_bc[0:nrow, :], op=ALU.mult))
                vop(lambda e: e.tensor_tensor(hs_dst, hs_dst, lnb_bc[0:nrow, :], op=ALU.add))

            emit_ln(halo3[:, :], 3, f32t[0:3, 0:H], f32t[0:3, H:2 * H],
                    w=[("dy", w0y), ("cv", p0_v), ("cs", p0_s)])
            vop(lambda e: e.tensor_scalar(f32t[0:3, H:2 * H], f32t[0:3, H:2 * H],
                                          flag_c[0:3, 0:1], None, op0=ALU.mult))
            vop(lambda e: e.tensor_copy(ar_a[0:3, 0:H], f32t[0:3, H:2 * H]))
            hv = B.v("cv")
            for j in range(8):
                top(lambda e, j=j: e.transpose(PS0b[:, j * 4:j * 4 + 3],
                                               ar_a[0:3, j * 128:(j + 1) * 128], identb[0:3, 0:3]),
                    waits=[("cv", hv)], inc=("ct" if j == 7 else None))
            vop(lambda e: e.tensor_copy(
                halo_fm[:, 0:24].rearrange("p (j k) -> p j k", j=8),
                PS0b[:, 0:32].rearrange("p (j k) -> p j k", j=8)[:, :, 0:3]),
                waits=[("ct", B.v("ct"))])
            halo_done_v = B.v("cv")

            evac_marks = [halo_done_v]
            for tc in range(NTC):
                xc_ap = xch[:, :]
                wv = [("cv", B.v("cv"))]
                dma("sync", xc_ap, x_loc[tc * 128:(tc + 1) * 128, :], waits=wv)
                wld = [("dy", B.v("dy")), ("cv", p0_v), ("cs", p0_s)]
                xn_f = f32t[:, 0:H]
                hs_f = f32t[:, H:2 * H]
                emit_ln(xc_ap, 128, xn_f, hs_f, w=wld)
                vop(lambda e, xc_ap=xc_ap, tc=tc: e.tensor_tensor(
                    xres[:, tc * H:(tc + 1) * H], xc_ap, xn_f, op=ALU.add))
                vop(lambda e, tc=tc: e.tensor_tensor(
                    xres[:, tc * H:(tc + 1) * H], xres[:, tc * H:(tc + 1) * H],
                    opb_bc[:, :], op=ALU.add))
                vop(lambda e: e.tensor_copy(ar_a[:, 0:H], xn_f), waits=[("ct", B.v("ct"))])
                vop(lambda e: e.tensor_copy(ar_a[:, H:2 * H], hs_f))
                vop(lambda e: e.tensor_tensor(ar_a[:, 2 * H:3 * H], xn_f, ar_a[:, 0:H],
                                              op=ALU.subtract))
                cast_v = B.v("cv")
                for so, po in ((0, 0), (H, 1024), (2 * H, 2048)):
                    for j in range(8):
                        top(lambda e, so=so, po=po, j=j: e.transpose(
                            PS0b[:, po + j * 128:po + (j + 1) * 128],
                            ar_a[:, so + j * 128:so + (j + 1) * 128], identb[:, :]),
                            waits=[("cv", cast_v)],
                            inc=("ct" if (so == 2 * H and j == 7) else None))
                tp_done = B.v("ct")
                for po, dstt in ((0, arn_c), (1024, buf_fm), (2048, ar_b)):
                    ew = [("ct", tp_done)] + ([("dc", apos_d)] if tc == 0 else [])
                    vop(lambda e, po=po, dstt=dstt, tc=tc: e.tensor_copy(
                        dstt[:, :].rearrange("q (j t) -> q j t", j=8)[:, :, tc * 128:(tc + 1) * 128],
                        PS0b[:, po:po + 1024].rearrange("q (j t) -> q j t", j=8)),
                        waits=ew)
                evac_marks.append(B.v("cv"))
            p1_v, p1_s, p1_t, p1_dy = B.v("cv"), B.v("cs"), B.v("ct"), B.v("dy")
            _ck(1)
            xn_fm, hs_fm, xnlo_fm = arn_c, buf_fm, ar_b

            for tt in range(4):
                combos = [(xn_fm, gw_hi), (xn_fm, gw_lo), (xnlo_fm, gw_hi)]
                n_mm = len(combos) * 8
                k = 0
                for xa, wa in combos:
                    for j in range(8):
                        top(lambda e, xa=xa, wa=wa, j=j, tt=tt, k=k, n=n_mm: e.matmul(
                            PS1[0:8, tt * 512:(tt + 1) * 512],
                            wa[:, j * 8:(j + 1) * 8],
                            xa[:, :].rearrange("q (jj t) -> q jj t", jj=8)[:, j, tt * 512:(tt + 1) * 512],
                            start=(k == 0), stop=(k == n_mm - 1)),
                            waits=[("cv", p1_v)],
                            inc=("ct" if k == n_mm - 1 else None))
                        k += 1
            sop(lambda e: e.activation(scoresS[:, :], PS1[0:8, :], AF.Sigmoid, bias=lb_sb[:, :]),
                waits=[("ct", B.v("ct")), ("dg", w0g)])
            sc_s = B.v("cs")
            if debug:
                dma("sync", dbg_sc, scoresS, waits=[("cs", sc_s)])
            idf32 = trif32
            vop(lambda e: e.tensor_copy(trif32[:, :], identb[:, :]), waits=[("ct", p1_t)])
            idf_v = B.v("cv")
            for tc in range(NTC):
                top(lambda e, tc=tc: e.transpose(PS0[:, tc * 8:(tc + 1) * 8],
                                                 scoresS[:, tc * 128:(tc + 1) * 128],
                                                 trif32[0:8, 0:8]),
                    waits=[("cs", sc_s), ("cv", idf_v)],
                    inc=("ct" if tc == NTC - 1 else None))
            vop(lambda e: e.tensor_copy(scT[:, :], PS0[:, 0:NTC * 8]), waits=[("ct", B.v("ct"))])
            scT_v = B.v("cv")
            if debug:
                dma("sync", dbg_sct, scT[:, :], waits=[("cv", scT_v)])

            epsc = stat[:, 8:9]
            vop(lambda e: e.memset(epsc, 1e-8), waits=[("cv", scT_v)], drain=True)
            for tc in range(NTC):
                sc = scT[:, tc * 8:(tc + 1) * 8]
                mx8 = smf[:, 0:8]; mrt = smf[:, 8:16]; diff = smf[:, 16:24]
                den = stat[:, 9:10]; den2 = stat[:, 10:11]; rec = stat[:, 11:12]
                vop(lambda e, sc=sc: e.max(mx8, sc), drain=True)
                vop(lambda e: e.memset(mx8[:, 2:8], -1.0), drain=True)
                vop(lambda e, sc=sc: e.match_replace(out=mrt, in_to_replace=mx8,
                                                     in_values=sc, imm_value=0.0),
                    drain=True)
                vop(lambda e, sc=sc: e.tensor_tensor(diff, sc, mrt, op=ALU.subtract),
                    drain=True)
                vop(lambda e: e.tensor_tensor(den, mx8[:, 0:1], mx8[:, 1:2], op=ALU.add),
                    drain=True)
                vop(lambda e: e.tensor_tensor(den2, den, epsc, op=ALU.add), drain=True)
                vop(lambda e: e.reciprocal(rec, den2), drain=True)
                vop(lambda e, tc=tc: e.tensor_scalar(
                    wtsel[:, tc * 16:tc * 16 + 8], diff, rec, None, op0=ALU.mult),
                    drain=True)
                vop(lambda e: e.tensor_scalar(smf[:, 24:32], diff, 1e20, None, op0=ALU.mult),
                    drain=True)
                vop(lambda e, tc=tc: e.tensor_scalar(
                    wtsel[:, tc * 16 + 8:tc * 16 + 16], smf[:, 24:32], 1.0, None,
                    op0=ALU.min), drain=True)
            wts_v = B.v("cv")
            if debug:
                dma("sync", dbg_wts, wtsel[:, :], waits=[("cv", wts_v)])
            for tc in range(NTC):
                top(lambda e, tc=tc: e.matmul(
                    PS0[0:1, 256 + tc * 16:256 + (tc + 1) * 16], ones128f[:, :],
                    wtsel[:, tc * 16:(tc + 1) * 16],
                    start=True, stop=True),
                    waits=[("cv", wts_v)], inc=("ct" if tc == NTC - 1 else None))
            vop(lambda e: e.tensor_reduce(
                rres[:, :],
                PS0[0:1, 256:512].rearrange("p (t k) -> p t k", t=16).transpose([0, 2, 1]),
                axis=AXX, op=ALU.add), waits=[("ct", B.v("ct"))], drain=True)
            dma("sync", r_in[:, :], rres[:, :], waits=[("cv", B.v("cv"))])
            gop(lambda e: e.collective_compute(
                "AllReduce", ALU.add, replica_groups=[[0, 1, 2, 3, 4, 5, 6, 7]],
                ins=[r_in.opt()], outs=[r_out.opt()]),
                waits=[("dy", B.v("dy"))], inc="cc")
            ar_done = B.v("cc")
            dma("sync", rres[:, :], r_out[:, :], waits=[("cc", ar_done)])
            vop(lambda e: e.tensor_scalar(stat[0:1, 12:13].transpose([0, 1]) if False
                                          else stat[0:1, 0:8], rres[0:1, 8:16], 1.0, None,
                                          op0=ALU.max),
                waits=[("dy", B.v("dy"))], drain=True)
            vop(lambda e: e.reciprocal(stat[0:1, 8:16], stat[0:1, 0:8]), drain=True)
            vop(lambda e: e.tensor_tensor(rres[0:1, 0:8], rres[0:1, 0:8], stat[0:1, 8:16],
                                          op=ALU.mult), drain=True)
            dma("scalar", wm_dram[:, :], rres[0:1, 0:8], waits=[("cv", B.v("cv"))])
            dma("sync", wm_bc[:, :], wm_dram[0, :].partition_broadcast(128),
                waits=[("dc", B.v("dc"))])
            wm_dy = B.v("dy")
            for tc in range(NTC):
                vop(lambda e, tc=tc: e.tensor_tensor(
                    selw8[:, :], wtsel[:, tc * 16 + 8:tc * 16 + 16], wm_bc[:, :], op=ALU.mult),
                    waits=[("dy", wm_dy)] + ([("ct", B.v("ct"))] if tc > 0 else []))
                svv = B.v("cv")
                top(lambda e, tc=tc: e.transpose(PS0b[0:8, 4096 - 128:4096], selw8[:, :],
                                                 identb[:, :]),
                    waits=[("cv", svv)], inc="ct")
                vop(lambda e: e.tensor_copy(swt_st[:, :], PS0b[0:8, 4096 - 128:4096]),
                    waits=[("ct", B.v("ct")), ("dc", B.v("dc"))])
                dma("scalar", swt_dram[:, tc * 128:(tc + 1) * 128], swt_st[:, :],
                    waits=[("cv", B.v("cv"))])
            swt_dc = B.v("dc")
            p2_v = B.v("cv")
            _ck(2)

            selw_bc = ar_b.ap()[:, 12288:14336]
            exw_g = lambda c, j: ar_b[:, (c * 8 + j) * 128:(c * 8 + j) * 128 + 128]
            exw_u = lambda c, j: ar_b[:, 4096 + (c * 8 + j) * 128:4096 + (c * 8 + j) * 128 + 128]
            exd_s = lambda c, half: ar_b[:, 8192 + c * 1024 + half * 512: 8192 + c * 1024 + half * 512 + 512]
            h_t = lambda ic, t0, t1: ar_a[:, ic * 2048 + t0:ic * 2048 + t1]

            last_down_t = B.v("ct")
            last_h_v = p2_v
            for ex in ([] if skip_ffn else range(10)):
                real = ex < 8
                wld_waits = [("ct", last_down_t), ("cv", p2_v)]
                if real:
                    src_g, src_u, src_d = egwt_r[ex], euwt_r[ex], edwt_r[ex]
                else:
                    half = ex - 8
                    src_g = sgwt_r[:, half * 4096:(half + 1) * 4096]
                    src_u = suwt_r[:, half * 4096:(half + 1) * 4096]
                    src_d = sdwt_r[:, half * 4096:(half + 1) * 4096]
                dma("gpsimd", ar_b[:, 0:4096], src_g, waits=wld_waits)
                dma("gpsimd", ar_b[:, 4096:8192], src_u, waits=wld_waits)
                dma("gpsimd", ar_b[:, 8192:12288], src_d, waits=wld_waits)
                if real:
                    dma("gpsimd", selw_bc[:, :], swt_dram[ex, :].partition_broadcast(128),
                        waits=[("dc", swt_dc), ("cv", last_h_v)])
                wl = B.v("dg")
                for ic in range(4):
                    for tt in range(4):
                        gq = PS0[:, (tt % 2) * 1024:(tt % 2) * 1024 + 512]
                        uq = PS0[:, (tt % 2) * 1024 + 512:(tt % 2) * 1024 + 1024]
                        mmw = [("dg", wl), ("cv", B.v("cv"))]
                        for j in range(8):
                            top(lambda e, gq=gq, ic=ic, j=j, tt=tt: e.matmul(
                                gq, exw_g(ic, j),
                                xn_fm[:, :].rearrange("q (jj t) -> q jj t", jj=8)[:, j, tt * 512:(tt + 1) * 512],
                                start=(j == 0), stop=(j == 7)), waits=mmw)
                        for j in range(8):
                            top(lambda e, uq=uq, ic=ic, j=j, tt=tt: e.matmul(
                                uq, exw_u(ic, j),
                                xn_fm[:, :].rearrange("q (jj t) -> q jj t", jj=8)[:, j, tt * 512:(tt + 1) * 512],
                                start=(j == 0), stop=(j == 7)),
                                waits=mmw, inc=("ct" if j == 7 else None))
                        guv = B.v("ct")
                        sop(lambda e, gq=gq: e.activation(f32t[:, 0:512], gq, AF.Silu),
                            waits=[("ct", guv), ("cv", B.v("cv"))])
                        vop(lambda e, uq=uq, ic=ic, tt=tt: e.tensor_tensor(
                            h_t(ic, tt * 512, (tt + 1) * 512), f32t[:, 0:512], uq, op=ALU.mult),
                            waits=[("cs", B.v("cs")), ("ct", last_down_t)])
                        if real:
                            vop(lambda e, ic=ic, tt=tt: e.tensor_tensor(
                                h_t(ic, tt * 512, (tt + 1) * 512),
                                h_t(ic, tt * 512, (tt + 1) * 512),
                                selw_bc[:, tt * 512:(tt + 1) * 512], op=ALU.mult),
                                waits=[("dg", wl)])
                last_h_v = B.v("cv")
                for tc in range(NTC):
                    dq = PS1[:, (tc % 2) * 1024:(tc % 2) * 1024 + 1024]
                    mmw = [("cv", last_h_v if tc == 0 else B.v("cv"))]
                    k = 0
                    for ic in range(4):
                        for hh in range(2):
                            top(lambda e, dq=dq, ic=ic, hh=hh, tc=tc, k=k: e.matmul(
                                dq[:, hh * 512:(hh + 1) * 512],
                                h_t(ic, tc * 128, (tc + 1) * 128),
                                exd_s(ic, hh),
                                start=(ic == 0), stop=(ic == 3)),
                                waits=mmw, inc=("ct" if k == 7 else None))
                            k += 1
                    dn = B.v("ct")
                    vop(lambda e, dq=dq, tc=tc: e.tensor_tensor(
                        xres[:, tc * H:(tc + 1) * H], dq, xres[:, tc * H:(tc + 1) * H],
                        op=ALU.add), waits=[("ct", dn)])
                last_down_t = B.v("ct")
            p3_v, p3_t = B.v("cv"), B.v("ct")
            _ck(3)
            if "spill" not in p4skip:
                for tc in range(NTC):
                    dma("sync", xres_dram[tc * 128:(tc + 1) * 128, :],
                        xres[:, tc * H:(tc + 1) * H], waits=[("cv", p3_v)])
            spill_dy = B.v("dy")
            if "wl1" not in p4skip:
                dma("gpsimd", dpw_sb, dpw[:, :], waits=[("dy", spill_dy)])
            if "wl2" not in p4skip:
                dma("gpsimd", dpb_sb, dpb_row[:, :], waits=[("dy", spill_dy)])
                dma("gpsimd", ipbg_row, ipb_row[0:1, E:E2], waits=[("dy", spill_dy)])
            if "wl3" not in p4skip:
                dma("gpsimd", xpw_sbv, xpw_r[:, :], waits=[("dy", spill_dy)])
            p4w_dg = B.v("dg")
            _ck(3.05)

            xpre_s = arn_c[:, 0:2048]
            convT = arn_c[:, 2048:4096]
            cat6 = arn_c[:, 4096:4160]
            dtst = arn_c[:, 6144:8192]
            xctst = arn_c[:, 8192:10240]

            def xc_t(ec, t0, t1):
                if ec < 8:
                    return ar_a[:, ec * 2048 + t0:ec * 2048 + t1]
                return ar_b[:, (ec - 8) * 2048 + t0:(ec - 8) * 2048 + t1]

            for ec in range(16):
                p = ec % 2
                ipd = ip_sbv[:, p * 1024:(p + 1) * 1024]
                dma("gpsimd", ipd, ipw_r[ec],
                    waits=[("ct", B.v("ct")), ("cv", (p3_v if ec == 0 else B.v("cv"))),
                           ("dy", spill_dy)])
                wl = B.v("dg")
                for j in (range(8) if "halo" not in p4skip else []):
                    top(lambda e, ipd=ipd, j=j: e.matmul(
                        PS0[:, 0:3], ipd[:, j * 128:(j + 1) * 128],
                        halo_fm[:, j * 3:j * 3 + 3], start=(j == 0), stop=(j == 7)),
                        waits=[("dg", wl), ("cv", B.v("cv")), ("cs", B.v("cs"))],
                        inc=("ct" if j == 7 else None))
                if "halo" not in p4skip:
                    vop(lambda e, ec=ec: e.tensor_scalar(
                        smf[:, 64:67], PS0[:, 0:3], ipb_sb[:, ec:ec + 1], flag_c[:, 0:1],
                        op0=ALU.add, op1=ALU.mult), waits=[("ct", B.v("ct"))])
                    vop(lambda e, ec=ec: e.tensor_copy(xpreh[:, ec * 4:ec * 4 + 3], smf[:, 64:67]))
                halo_ev = B.v("cv")
                for tt in range(4):
                    q = PS0[:, tt * 512:(tt + 1) * 512]
                    for j in range(8):
                        top(lambda e, q=q, ipd=ipd, j=j, tt=tt: e.matmul(
                            q, ipd[:, j * 128:(j + 1) * 128],
                            hs_fm[:, :].rearrange("w (jj t) -> w jj t", jj=8)[:, j, tt * 512:(tt + 1) * 512],
                            start=(j == 0), stop=(j == 7)),
                            waits=[("cv", halo_ev)], inc=("ct" if j == 7 else None))
                    vop(lambda e, q=q, ec=ec, tt=tt: e.tensor_scalar(
                        xpre_s[:, tt * 512:(tt + 1) * 512], q, ipb_sb[:, ec:ec + 1], None,
                        op0=ALU.add), waits=[("ct", B.v("ct"))])
                xp_v = B.v("cv")
                if "conv" in p4skip:
                    vop(lambda e, ec=ec: e.tensor_copy(convT[:, :], xpre_s[:, :]))
                wk = lambda k: cw_sb[:, ec * 4 + k:ec * 4 + k + 1]
                if "conv" in p4skip:
                    pass
                if "conv" not in p4skip:
                    vop(lambda e, ec=ec: e.tensor_scalar(
                        convT[:, 3:2048], xpre_s[:, 0:2045], wk(0), None, op0=ALU.mult))
                    vop(lambda e, ec=ec: e.scalar_tensor_tensor(
                        convT[:, 3:2048], xpre_s[:, 1:2046], wk(1), convT[:, 3:2048],
                        op0=ALU.mult, op1=ALU.add))
                    vop(lambda e, ec=ec: e.scalar_tensor_tensor(
                        convT[:, 3:2048], xpre_s[:, 2:2047], wk(2), convT[:, 3:2048],
                        op0=ALU.mult, op1=ALU.add))
                    vop(lambda e, ec=ec: e.scalar_tensor_tensor(
                        convT[:, 3:2048], xpre_s[:, 3:2048], wk(3), convT[:, 3:2048],
                        op0=ALU.mult, op1=ALU.add))
                    vop(lambda e, ec=ec: e.tensor_copy(cat6[:, 0:3], xpreh[:, ec * 4:ec * 4 + 3]))
                    vop(lambda e: e.tensor_copy(cat6[:, 3:6], xpre_s[:, 0:3]))
                    vop(lambda e, ec=ec: e.tensor_scalar(
                        convT[:, 0:3], cat6[:, 0:3], wk(0), None, op0=ALU.mult))
                    vop(lambda e, ec=ec: e.scalar_tensor_tensor(
                        convT[:, 0:3], cat6[:, 1:4], wk(1), convT[:, 0:3], op0=ALU.mult, op1=ALU.add))
                    vop(lambda e, ec=ec: e.scalar_tensor_tensor(
                        convT[:, 0:3], cat6[:, 2:5], wk(2), convT[:, 0:3], op0=ALU.mult, op1=ALU.add))
                    vop(lambda e, ec=ec: e.scalar_tensor_tensor(
                        convT[:, 0:3], cat6[:, 3:6], wk(3), convT[:, 0:3], op0=ALU.mult, op1=ALU.add))
                cv_conv = B.v("cv")
                sop(lambda e, ec=ec: e.activation(xc_t(ec, 0, 2048), convT[:, :], AF.Silu,
                                                  bias=cb_sb[:, ec:ec + 1]),
                    waits=[("cv", cv_conv)])
                xc_done = B.v("cs")
                for tt in (range(4) if "xproj" not in p4skip else []):
                    top(lambda e, ec=ec, tt=tt: e.matmul(
                        PS1[0:96, tt * 512:(tt + 1) * 512],
                        xpw_sbv[:, ec * 96:(ec + 1) * 96],
                        xc_t(ec, tt * 512, (tt + 1) * 512),
                        start=(ec == 0), stop=(ec == 15)),
                        waits=[("cs", xc_done), ("dg", p4w_dg)],
                        inc=("ct" if (ec == 15 and tt == 3) else None))
            _ck(3.2)
            dbc_t = B.v("ct")
            vop(lambda e: e.tensor_copy(dtr_sbv, PS1[0:64, :]), waits=[("ct", dbc_t)])
            vop(lambda e: e.tensor_copy(bc_allv, PS1[64:96, :]))
            dbc_ev = B.v("cv")
            for tc in range(NTC):
                top(lambda e, tc=tc: e.transpose(
                    PS0b[:, 2048:2080], bc_allv[:, tc * 128:(tc + 1) * 128], bcperm_sb[:, :]),
                    waits=[("cv", dbc_ev if tc == 0 else B.v("cv")), ("dy", B.v("dy"))],
                    inc="ct")
                vop(lambda e: e.tensor_copy(bcst[:, :], PS0b[:, 2048:2080]),
                    waits=[("ct", B.v("ct"))])
                dma("sync", agbc_in[tc * 128:(tc + 1) * 128, :], bcst[:, :],
                    waits=[("cv", B.v("cv"))])
            _ck(3.4)
            for tc in range(NTC):
                for tt in range(4):
                    top(lambda e, tc=tc, tt=tt: e.matmul(
                        PS1[:, tt * 512:(tt + 1) * 512],
                        dtr_sbv[:, tc * 128:(tc + 1) * 128],
                        dpw_sb[:, tt * 512:(tt + 1) * 512],
                        start=True, stop=False),
                        waits=[("cs", B.v("cs")), ("dy", B.v("dy")), ("cv", dbc_ev),
                               ("dg", p4w_dg)])
                    top(lambda e, tc=tc, tt=tt: e.matmul(
                        PS1[:, tt * 512:(tt + 1) * 512],
                        ones1b[:, :],
                        dpb_sb[:, tt * 512:(tt + 1) * 512],
                        start=False, stop=True),
                        inc=("ct" if tt == 3 else None))
                dtp_t = B.v("ct")
                sop(lambda e: e.activation(sg1p4, PS1[:, :], AF.Sigmoid, scale=-1.0),
                    waits=[("ct", dtp_t)])
                sop(lambda e: e.activation(dtst[:, :], sg1p4, AF.Ln), inc="cs")
                dma("sync", ag_in[tc * 128:(tc + 1) * 128, 0:2048], dtst[:, :],
                    waits=[("cs", B.v("cs"))])
            _ck(3.6)
            for tc in range(NTC):
                for ec in range(16):
                    top(lambda e, tc=tc, ec=ec: e.transpose(
                        PS0b[:, ec * 128:(ec + 1) * 128],
                        xc_t(ec, tc * 128, (tc + 1) * 128), identb[:, :]),
                        waits=[("dy", B.v("dy"))], inc=("ct" if ec == 15 else None))
                vop(lambda e: e.tensor_copy(xctst[:, :], PS0b[:, 0:2048]),
                    waits=[("ct", B.v("ct"))])
                dma("sync", ag_in[tc * 128:(tc + 1) * 128, 2048:4096], xctst[:, :],
                    waits=[("cv", B.v("cv"))])
            p4_dy, p4_v, p4_s, p4_t = B.v("dy"), B.v("cv"), B.v("cs"), B.v("ct")
            _ck(4)

            gop(lambda e: e.collective_compute(
                "AllGather", ALU.bypass, replica_groups=[[0, 1], [2, 3], [4, 5], [6, 7]],
                ins=[ag_in.opt()], outs=[ag_out.opt()]),
                waits=[("dy", p4_dy)], inc="cc")
            _ck(4.5)
            gop(lambda e: e.collective_compute(
                "AllGather", ALU.bypass, replica_groups=[[0, 1], [2, 3], [4, 5], [6, 7]],
                ins=[agbc_in.opt()], outs=[agbc_out.opt()]),
                inc="cc")
            ag_done = B.v("cc")
            _ck(5)

            for eti in range(4):
                for sub in range(4):
                    dma("gpsimd", wg_sbv[:, sub * 1024:(sub + 1) * 1024],
                        ipw_r[16 + eti * 4 + sub], waits=[("ct", B.v("ct"))])
                wl = B.v("dg")
                for tc in range(NTC):
                    q = PS0[:, (tc % 2) * 512:(tc % 2) * 512 + 512]
                    k = 0
                    for j in range(8):
                        top(lambda e, q=q, j=j, tc=tc: e.matmul(
                            q, hs_fm[:, j * TOK + tc * 128:j * TOK + (tc + 1) * 128],
                            wg_sbv.rearrange("w (jj m) -> w jj m", jj=8)[:, j, :],
                            start=(j == 0), stop=False),
                            waits=[("dg", wl), ("cs", B.v("cs"))])
                    top(lambda e, q=q, eti=eti: e.matmul(
                        q, ones1b[:, :], ipbg_row[0:1, eti * 512:(eti + 1) * 512],
                        start=False, stop=True), inc="ct")
                    sop(lambda e, q=q: e.activation(f32t[:, 0:512], q, AF.Sigmoid),
                        waits=[("ct", B.v("ct"))])
                    sop(lambda e: e.copy(convT[:, 0:512], f32t[:, 0:512]),
                        waits=[("dy", B.v("dy"))], inc="cs")
                    dma("sync", sg_dram[tc * 128:(tc + 1) * 128, eti * 512:(eti + 1) * 512],
                        convT[:, 0:512], waits=[("cs", B.v("cs"))])
            p6_t, p6_dy = B.v("ct"), B.v("dy")
            _ck(6)

            for n in range(NL):
                dma("gpsimd", buf_fm[:, n * E:(n + 1) * E],
                    apos_dram[n, :].partition_broadcast(128),
                    waits=[("ct", p6_t), ("dc", apos_d)])
            abc_dg = B.v("dg")
            A_bc = buf_fm

            z_t = [ar_a[:, 0:2048], arn_c[:, 2112:4160]]
            e1_t = [ar_a[:, 2048:4096], ar_a[:, 4096:6144], arn_c[:, 6208:8256]]
            e2_t = [ar_a[:, 6144:8192], arn_c[:, 4160:6208]]
            v_t = [ar_a[:, 8192:10240], ar_a[:, 10240:12288]]
            eS_t = [ar_a[:, 12288:14336], ar_a[:, 14336:16384]]
            ndt_t = [ar_b[:, 0:2048], ar_b[:, 2048:4096]]
            xct_t = [ar_b[:, 4096:6144], ar_b[:, 6144:8192]]
            nDc_t = ar_b[:, 8192:10240]
            u_t = ar_b[:, 10240:12288]
            y_t = [ar_b[:, 12288:14336], ar_b[:, 14336:16384]]
            ybase = arn_c[:, 0:2048]
            bc32 = [arn_c[:, 2048:2080], arn_c[:, 2080:2112]]
            bcu = smf.ap()[:, 64:80]
            f32b = f32t.ap().bitcast(BF16)
            hrowp = [f32b[0:NL, 0:2048], f32b[0:NL, 2048:4096]]
            v0c = f32b[0:NL, 4096:6144]
            dcm_bc = xrb[0:NL, 22528:24576]
            z0c = xrb[0:NL, 24576:26624]
            e20c = xrb[0:NL, 26624:28672]
            dma("gpsimd", dhalf_bc, d_row[0, :].partition_broadcast(128),
                waits=[("dy", spill_dy), ("ct", B.v("ct")), ("cv", B.v("cv"))])
            vop(lambda e: e.tensor_scalar(dhalf_bc, dhalf_bc, 0.5, None, op0=ALU.mult),
                waits=[("dg", B.v("dg"))])
            dma("sync", aposv, apos_dram[:, :], waits=[("dc", apos_d), ("dy", B.v("dy"))])
            aposv_dy = B.v("dy")

            vop(lambda e: e.memset(ndt_t[0][:, :], 0.0), waits=[("ct", p4_t), ("dy", p6_dy),
                                                                ("cs", B.v("cs"))])
            vop(lambda e: e.memset(hrowp[0], 0.0))
            vop(lambda e: e.memset(hrowp[1], 0.0))
            vop(lambda e: e.memset(ndt_t[1][:, :], 0.0))
            vop(lambda e: e.memset(xct_t[0][:, :], 0.0))
            vop(lambda e: e.memset(xct_t[1][:, :], 0.0))
            vop(lambda e: e.memset(bc32[0][:, :], 0.0))
            vop(lambda e: e.memset(bc32[1][:, :], 0.0))
            zinit_v = B.v("cv")
            p7pre_s, p7pre_v = B.v("cs"), B.v("cv")

            ld_mark = {}

            def emit_load(cc, war):
                p_ = cc % 2
                ns_ = min(TC, SEQ - cc * TC)
                r0_ = cc * TC
                w = list(war)
                if ns_ < TC:
                    vop(lambda e, p_=p_: e.memset(ndt_t[p_][:, :], 0.0), waits=war)
                    vop(lambda e, p_=p_: e.memset(xct_t[p_][:, :], 0.0))
                    vop(lambda e, p_=p_: e.memset(bc32[p_][:, :], 0.0))
                    w = [("cv", B.v("cv")), ("cc", ag_done)]
                dma("sync", ndt_t[p_][1:1 + ns_, :], ag_out[r0_:r0_ + ns_, 0:2048], waits=w)
                dma("sync", xct_t[p_][1:1 + ns_, :], ag_out[r0_:r0_ + ns_, 2048:4096], waits=w)
                dma("sync", bc32[p_][1:1 + ns_, :], agbc_out[r0_:r0_ + ns_, :], waits=w)
                ld_mark[cc] = B.v("dy")

            emit_load(0, [("cv", zinit_v), ("cc", ag_done)])
            emit_load(1, [("cv", zinit_v), ("cc", ag_done)])

            chunk_end = []
            csm, ctm = [], []
            es_prev, ex_prev = {}, {}
            ystore_prev = None
            last_z_cv = None
            st_dy_chunk = {}
            for c in range(NCH):
                p = c % 2
                ns = min(TC, SEQ - c * TC)
                r0 = c * TC
                ld = ld_mark[c]
                vop(lambda e, p=p: e.tensor_scalar(
                    bcu[:, 0:8], bc32[p][:, 0:8], ab_col[:, 0:1], None, op0=ALU.mult),
                    waits=[("dy", ld)])
                vop(lambda e, p=p: e.scalar_tensor_tensor(
                    bcu[:, 0:8], bc32[p][:, 8:16], ab_col[:, 1:2], bcu[:, 0:8],
                    op0=ALU.mult, op1=ALU.add))
                vop(lambda e, p=p: e.tensor_scalar(
                    bcu[:, 8:16], bc32[p][:, 16:24], ab_col[:, 0:1], None, op0=ALU.mult))
                vop(lambda e, p=p: e.scalar_tensor_tensor(
                    bcu[:, 8:16], bc32[p][:, 24:32], ab_col[:, 1:2], bcu[:, 8:16],
                    op0=ALU.mult, op1=ALU.add))
                vop(lambda e, p=p: e.tensor_tensor(u_t[:, :], ndt_t[p][:, :], xct_t[p][:, :],
                                                   op=ALU.mult))
                vop(lambda e, p=p: e.tensor_tensor(ybase[:, :], xct_t[p][:, :], dhalf_bc[:, :],
                                                   op=ALU.mult))
                pre_v = B.v("cv")
                dcw = [("dy", ld)]
                dcw += [("cv", es_prev[6])] if c > 0 else \
                       [("cv", p7pre_v), ("cs", p7pre_s)]
                for tt in range(4):
                    top(lambda e, p=p, tt=tt: e.matmul(
                        PS1[:, tt * 512:(tt + 1) * 512], tric[:, :],
                        ndt_t[p][:, tt * 512:(tt + 1) * 512], start=True, stop=True),
                        waits=dcw, inc=("ct" if tt == 3 else None))
                dcmm_t = B.v("ct")
                ncw = [("ct", dcmm_t)]
                if c > 0:
                    ncw.append(("cv", last_z_cv))
                sop(lambda e: e.copy(nDc_t[:, :], PS1[:, :]), waits=ncw, inc="cs")
                ndc_s = B.v("cs")
                if c + 2 < NCH:
                    emit_load(c + 2, [("cv", pre_v), ("ct", dcmm_t), ("cc", ag_done)])
                if c > 0:
                    dma("scalar", dcm_dram[:, :], nDc_t[0:1, :], waits=[("cs", ndc_s)])
                    dma("scalar", dcm_bc[:, :], dcm_dram[0, :].partition_broadcast(NL),
                        waits=[("dc", B.v("dc"))])
                    dcm_d = B.v("dc")
                stz, cs_n, vv_n, ct_n, es_n, ex_n = {}, {}, {}, {}, {}, {}
                for k in range(NL + 3):
                    if k == 2 and c > 0:
                        vop(lambda e: e.tensor_tensor(z0c, aposv, dcm_bc, op=ALU.mult),
                            waits=[("dc", dcm_d), ("dy", aposv_dy)])
                        sop(lambda e: e.activation(e20c, z0c, AF.Exp, scale=-1.0),
                            waits=[("cv", B.v("cv"))])
                        vop(lambda e, c=c: e.tensor_tensor(
                            v0c, e20c, hrowp[(c - 1) % 2], op=ALU.mult),
                            waits=[("cs", B.v("cs")), ("dc", ex_prev[7])])
                    if k < NL:
                        n, np_ = k, k % 2
                        zw = [("cs", ndc_s), ("dg", abc_dg)]
                        if len(csm) >= 2:
                            zw.append(("cs", csm[-2]))
                        vop(lambda e, n=n, np_=np_: e.tensor_tensor(
                            z_t[np_][:, :], nDc_t[:, :], A_bc[:, n * E:(n + 1) * E],
                            op=ALU.mult), waits=zw)
                        stz[n] = B.v("cv")
                        if n == NL - 1:
                            last_z_cv = stz[n]
                    if 1 <= k <= NL:
                        n, np_ = k - 1, (k - 1) % 2
                        n3 = n % 3
                        if n >= 3:
                            ew = [("cv", es_n[n - 3])]
                        elif n >= 2:
                            ew = [("cv", vv_n[n - 2])]
                        else:
                            ew = [("cv", stz[n])]
                        sop(lambda e, np_=np_, n3=n3: e.activation(
                            e1_t[n3][:, :], z_t[np_][:, :], AF.Exp), waits=ew)
                        sop(lambda e, np_=np_: e.activation(
                            e2_t[np_][:, :], z_t[np_][:, :], AF.Exp, scale=-1.0))
                        cs_n[n] = B.v("cs")
                        csm.append(cs_n[n])
                    if 2 <= k <= NL + 1:
                        n, np_ = k - 2, (k - 2) % 2
                        vw = [("cs", cs_n[n])]
                        if len(ctm) >= 2:
                            vw.append(("ct", ctm[-2]))
                        vop(lambda e, np_=np_, n=n: e.scalar_tensor_tensor(
                            v_t[np_][:, :], u_t[:, :], bcu[:, n:n + 1], e2_t[np_][:, :],
                            op0=ALU.mult, op1=ALU.mult), waits=vw)
                        vv_n[n] = B.v("cv")
                        mmw = [("cv", vv_n[n])]
                        if c > 0:
                            dma("scalar", v_t[np_][0:1, :], v0c[n:n + 1, :],
                                waits=[("cv", vv_n[n])])
                            mmw.append(("dc", B.v("dc")))
                        if n >= 2:
                            mmw.append(("cv", es_n[n - 2]))
                        elif n == 0:
                            mmw.append(("cs", ndc_s))
                        elif c > 0:
                            mmw.append(("cv", es_prev[7]))
                        else:
                            mmw += [("cv", p7pre_v), ("cs", p7pre_s)]
                        PSn = PS1 if np_ == 0 else PS0
                        for tt in range(4):
                            top(lambda e, PSn=PSn, np_=np_, tt=tt: e.matmul(
                                PSn[:, tt * 512:(tt + 1) * 512], trip[:, :],
                                v_t[np_][:, tt * 512:(tt + 1) * 512],
                                start=True, stop=True),
                                waits=mmw, inc=("ct" if tt == 3 else None))
                        ct_n[n] = B.v("ct")
                        ctm.append(ct_n[n])
                    if k >= 3:
                        n, np_ = k - 3, (k - 3) % 2
                        n3 = n % 3
                        PSn = PS1 if np_ == 0 else PS0
                        ew2 = [("ct", ct_n[n]), ("cs", cs_n[n])]
                        if n >= 2:
                            ew2.append(("dc", ex_n[n - 2]))
                        elif c > 0:
                            ew2.append(("dc", ex_prev[6 + n]))
                        vop(lambda e, PSn=PSn, np_=np_, n3=n3: e.tensor_tensor(
                            eS_t[np_][:, :], e1_t[n3][:, :], PSn[:, :], op=ALU.mult),
                            waits=ew2)
                        es_n[n] = B.v("cv")
                        dma("scalar", hrowp[c % 2][n:n + 1, :],
                            eS_t[np_][ns:ns + 1, :], waits=[("cv", es_n[n])])
                        ex_n[n] = B.v("dc")
                        prev = ybase if n == 0 else y_t[(n - 1) % 2]
                        yw = []
                        if n == 1 and ystore_prev is not None:
                            yw.append(("dy", ystore_prev))
                        vop(lambda e, np_=np_, n=n, prev=prev: e.scalar_tensor_tensor(
                            y_t[np_][:, :], eS_t[np_][:, :], bcu[:, 8 + n:9 + n], prev,
                            op0=ALU.mult, op1=ALU.add), waits=yw)
                yv = B.v("cv")
                pts = sorted({r0, r0 + ns} | {b for b in range(512, SEQ, 512)
                                              if r0 < b < r0 + ns})
                for a, b in zip(pts[:-1], pts[1:]):
                    half, r = a // 2048, a % 2048
                    q, off = r // 512, r % 512
                    dst = y_inQ[q][half * 512 + off:half * 512 + off + (b - a), :]
                    dma("sync", dst, y_t[(NL - 1) % 2][1 + (a - r0):1 + (b - r0), :],
                        waits=[("cv", yv)])
                st_dy_chunk[c] = B.v("dy")
                ystore_prev = B.v("dy")
                es_prev, ex_prev = es_n, ex_n
                chunk_end.append({"cv": B.v("cv"), "ct": B.v("ct"), "dy": B.v("dy"),
                                  "dc": B.v("dc"), "cs": B.v("cs")})
            p7 = chunk_end[-1]
            _ck(7)

            rs_marks = []
            for q, lastc in ((0, 20), (1, 24), (2, 28)):
                gop(lambda e, q=q: e.collective_compute(
                    "ReduceScatter", ALU.add,
                    replica_groups=[[0, 1], [2, 3], [4, 5], [6, 7]],
                    ins=[y_inQ[q].opt()], outs=[y_outQ[q].opt()]),
                    waits=[("dy", st_dy_chunk[lastc])], inc="cc")
                rs_marks.append(B.v("cc"))
            for ec in range(16):
                dma("gpsimd", buf_fm[:, ec * 1024:(ec + 1) * 1024],
                    opw_d[ec * 128:(ec + 1) * 128, :],
                    waits=[("cv", last_z_cv)])
            opw_dg = B.v("dg")
            opw_sb = buf_fm
            gop(lambda e: e.collective_compute(
                "ReduceScatter", ALU.add, replica_groups=[[0, 1], [2, 3], [4, 5], [6, 7]],
                ins=[y_inQ[3].opt()], outs=[y_outQ[3].opt()]),
                waits=[("dy", p7["dy"])], inc="cc")
            rs_marks.append(B.v("cc"))
            _ck(8)

            for tc in range(NTC):
                p = tc % 2
                sgt = ar_a[:, p * 2048:p * 2048 + 2048]
                yft = ar_a[:, 4096 + p * 2048:4096 + p * 2048 + 2048]
                ytot = ar_a[:, 8192 + p * 2048:8192 + p * 2048 + 2048]
                yEst = arn_c[:, 4096 + p * 2048:4096 + p * 2048 + 2048]
                war = [("cv", B.v("cv")), ("ct", B.v("ct"))] if tc >= 2 else \
                      [("cv", p7["cv"]), ("ct", p7["ct"])]
                dma("sync", sgt, sg_dram[tc * 128:(tc + 1) * 128, :], waits=war + [("dy", p6_dy)])
                dma("sync", yft, y_outQ[tc // 4][(tc % 4) * 128:(tc % 4 + 1) * 128, :],
                    waits=war + [("cc", rs_marks[tc // 4])])
                ld = B.v("dy")
                vop(lambda e, sgt=sgt, yft=yft, ytot=ytot: e.tensor_tensor(
                    ytot, sgt, yft, op=ALU.mult), waits=[("dy", ld)])
                ytv = B.v("cv")
                for ec in range(16):
                    top(lambda e, ytot=ytot, ec=ec: e.transpose(
                        PS0b[:, ec * 128:(ec + 1) * 128],
                        ytot[:, ec * 128:(ec + 1) * 128], identb[:, :]),
                        waits=[("cv", ytv)], inc=("ct" if ec == 15 else None))
                vop(lambda e, yEst=yEst: e.tensor_copy(yEst, PS0b[:, 0:2048]),
                    waits=[("ct", B.v("ct"))])
                yev = B.v("cv")
                for hh in range(2):
                    q = PS1[:, (tc % 2) * 1024 + hh * 512:(tc % 2) * 1024 + (hh + 1) * 512]
                    for ec in range(16):
                        top(lambda e, q=q, yEst=yEst, ec=ec, hh=hh: e.matmul(
                            q, yEst[:, ec * 128:(ec + 1) * 128],
                            opw_sb[:, ec * 1024 + hh * 512:ec * 1024 + (hh + 1) * 512],
                            start=(ec == 0), stop=(ec == 15)),
                            waits=[("cv", yev), ("dg", opw_dg)],
                            inc=("ct" if ec == 15 else None))
                dma("sync", f32t[:, 2 * H:3 * H], xres_dram[tc * 128:(tc + 1) * 128, :],
                    waits=[("cv", B.v("cv"))])
                xin_dy = B.v("dy")
                opt = B.v("ct")
                vop(lambda e, tc=tc, p=p: e.tensor_tensor(
                    f32t[:, p * H:(p + 1) * H], PS1[:, (tc % 2) * 1024:(tc % 2) * 1024 + 1024],
                    f32t[:, 2 * H:3 * H], op=ALU.add),
                    waits=[("ct", opt), ("dy", xin_dy)])
                dma("sync", out_loc[tc * 128:(tc + 1) * 128, :], f32t[:, p * H:(p + 1) * H],
                    waits=[("cv", B.v("cv"))])
            B.emit("sync", lambda e: e.nop(), [("dy", B.v("dy"))], None)


        except _StopBuild:
            partial[0] = True
        if debug:
            for nm, src in ([("sb_arb", ar_b.ap()), ("sb_arnc", arn_c.ap()),
                             ("sb_buffm", buf_fm.ap()), ("sb_gwhi", gw_hi.ap()),
                             ("sb_gwlo", gw_lo.ap()), ("sb_gwf", gw_f.ap())]
                            if upto <= 2 else []) + [
                            ("xres_dram", xres_dram), ("ag_out", ag_out),
                            ("agbc_out", agbc_out), ("y_inQ0", y_inQ[0]),
                            ("y_outQ0", y_outQ[0]),
                            ("sg_dram", sg_dram), ("swt_dram", swt_dram),
                            ("wm_dram", wm_dram), ("r_out", r_out),
                            ("apos_dram", apos_dram), ("ag_in", ag_in), ("r_in", r_in),
                            ("agbc_in", agbc_in)]:
                try:
                    d = nc.dram_tensor("dbg_" + nm, list(src.shape), src.dtype,
                                       kind="ExternalOutput").ap()
                    allw = [(sn, B.v(sn)) for sn in
                            ("dy", "dc", "dg", "cv", "cs", "ct", "cg", "cc") if B.v(sn) > 0]
                    dma("sync", d, src, waits=allw)
                except Exception:
                    pass
            B.emit("sync", lambda e: e.nop(), [("dy", B.v("dy"))], None)
        if partial[0]:
            allw = [(sn, B.v(sn)) for sn in ("dy", "dc", "dg", "cv", "cs", "ct", "cg", "cc") if B.v(sn) > 0]
            for tc in range(NTC):
                dma("sync", out_loc[tc * 128:(tc + 1) * 128, :], f32t[:, 0:H], waits=allw)
            B.emit("sync", lambda e: e.nop(), [("dy", B.v("dy"))], None)

        COMPUTE_SEMS = {"cv", "cs", "ct", "cg"}
        waited = defaultdict(set)
        for eng in ENGS:
            for fn, waits, inc, amt in B.ops[eng]:
                for sname, vv in waits:
                    waited[sname].add(vv)
        for eng in ENGS:
            cum = defaultdict(int)
            pend = defaultdict(int)
            newops = []
            for fn, waits, inc, amt in B.ops[eng]:
                inc2, amt2 = inc, amt
                if inc is not None and inc in COMPUTE_SEMS:
                    cum[inc] += amt
                    pend[inc] += amt
                    if cum[inc] in waited[inc]:
                        inc2, amt2 = inc, pend[inc]
                        pend[inc] = 0
                    else:
                        inc2 = None
                newops.append((fn, waits, inc2, amt2))
            B.ops[eng] = newops

        def run_stream(eng_handle, ops):
            hw = defaultdict(int)
            for fn, waits, inc, amt in ops:
                for s, vv in waits:
                    if vv > hw[s]:
                        eng_handle.wait_ge(sems[s], vv)
                        hw[s] = vv
                instr = fn(eng_handle)
                if inc is not None:
                    instr.then_inc(sems[inc], amt, skip_validation=True)

        @blk.sync
        def _(eng):
            run_stream(eng, B.ops["sync"])

        @blk.scalar
        def _(eng):
            run_stream(eng, B.ops["scalar"])

        @blk.vector
        def _(eng):
            run_stream(eng, B.ops["vector"])

        @blk.tensor
        def _(eng):
            run_stream(eng, B.ops["tensor"])

        @blk.gpsimd
        def _(eng):
            run_stream(eng, B.ops["gpsimd"])

    return nc


_NC_CACHE = None


def _get_nc():
    global _NC_CACHE
    if _NC_CACHE is None:
        _NC_CACHE = build_nc()
    return _NC_CACHE


def kernel(**inputs):
    f32 = lambda k: np.ascontiguousarray(np.asarray(inputs[k], dtype=np.float32))
    x = f32("L0_attn_hidden_states")
    ipw_h = f32("L0_attn_in_proj_w")
    ipw_r_h = np.ascontiguousarray(
        ipw_h.reshape(8, 128, 32, 128).transpose(2, 1, 0, 3).reshape(32, 128, 1024))
    ipb_full = f32("L0_attn_in_proj_b")
    ipb_col_h = np.ascontiguousarray(ipb_full[0:E].reshape(16, 128).T)
    cw_full = f32("L0_attn_conv_w")
    cw_col_h = np.ascontiguousarray(cw_full.reshape(16, 128, 4).transpose(1, 0, 2).reshape(128, 64))
    cb_col_h = np.ascontiguousarray(f32("L0_attn_conv_b").reshape(16, 128).T)
    xpw_h = f32("L0_attn_x_proj_w")
    xpw_r_h = np.ascontiguousarray(
        xpw_h.reshape(16, 128, 96).transpose(1, 0, 2).reshape(128, 16 * 96))
    dpw_h = f32("L0_attn_dt_proj_w")
    dpb_h = f32("L0_attn_dt_proj_b").reshape(1, E)
    alog_t = np.ascontiguousarray(f32("L0_attn_A_log").T)
    d_h = f32("L0_attn_D").reshape(1, E)
    opw_h = f32("L0_attn_out_proj_w")
    opb_h = f32("L0_attn_out_proj_b").reshape(1, H)
    lnw_h = f32("L0_attn_ln_w").reshape(1, H)
    lnb_h = f32("L0_attn_ln_b").reshape(1, H)
    gwt_h = np.ascontiguousarray(f32("L0_ffn_gate_w").T)
    gwt_r_h = np.ascontiguousarray(gwt_h.reshape(8, 128, 8).transpose(1, 0, 2).reshape(128, 64))
    lb_h = f32("L0_ffn_lb_bias").reshape(8, 1)
    def _exp_gu(w):
        wt = w.transpose(0, 2, 1)
        return np.ascontiguousarray(
            wt.reshape(8, 8, 128, 4, 128).transpose(0, 2, 3, 1, 4).reshape(8, 128, 4096))

    egwt_r_h = _exp_gu(f32("L0_ffn_expert_gate_w"))
    euwt_r_h = _exp_gu(f32("L0_ffn_expert_up_w"))
    edwt_t = f32("L0_ffn_expert_down_w").transpose(0, 2, 1)
    edwt_r_h = np.ascontiguousarray(
        edwt_t.reshape(8, 4, 128, 1024).transpose(0, 2, 1, 3).reshape(8, 128, 4096))

    def _sh_gu(w):
        wt = np.ascontiguousarray(w.T)
        return np.ascontiguousarray(
            wt.reshape(8, 128, 2, 4, 128).transpose(1, 2, 3, 0, 4).reshape(128, 8192))

    sgwt_r_h = _sh_gu(f32("L0_ffn_shared_gate_w"))
    suwt_r_h = _sh_gu(f32("L0_ffn_shared_up_w"))
    sdwt_t = np.ascontiguousarray(f32("L0_ffn_shared_down_w").T)
    sdwt_r_h = np.ascontiguousarray(
        sdwt_t.reshape(2, 4, 128, 1024).transpose(2, 0, 1, 3).reshape(128, 8192))
    bcperm_h = np.eye(32, dtype=np.float32)

    in_maps = []
    for c in range(8):
        b, h = c // 2, c % 2
        xl = np.ascontiguousarray(x[b, h * TOK:(h + 1) * TOK, :])
        if h == 1:
            xh = np.ascontiguousarray(x[b, TOK - 3:TOK, :])
            hf = np.ones((1, 1), np.float32)
        else:
            xh = np.zeros((3, H), np.float32)
            hf = np.zeros((1, 1), np.float32)
        alog_loc = np.ascontiguousarray(alog_t[h * NL:(h + 1) * NL, :])
        ab = np.array([[1.0 - h, float(h), -(1.0 - h), -float(h)]], np.float32)
        in_maps.append({
            "x_loc": xl, "x_halo": xh, "haloflag": hf,
            "ln_w": lnw_h, "ln_b": lnb_h,
            "ipw_r": ipw_r_h, "ipb_col": ipb_col_h, "ipb_row": ipb_full.reshape(1, E2),
            "cw_col": cw_col_h, "cb_col": cb_col_h, "xpw_r": xpw_r_h,
            "dpw": dpw_h, "dpb_row": dpb_h,
            "a_logt": alog_loc, "d_row": d_h,
            "opw": opw_h, "opb_row": opb_h,
            "gwt_r": gwt_r_h, "lb": lb_h,
            "egwt_r": egwt_r_h, "euwt_r": euwt_r_h, "edwt_r": edwt_r_h,
            "sgwt_r": sgwt_r_h, "suwt_r": suwt_r_h, "sdwt_r": sdwt_r_h,
            "bcperm": bcperm_h, "alphab": ab,
        })
    nc = _get_nc()
    res = run_bass_kernel_spmd(nc, in_maps, core_ids=list(range(8)))
    out = np.empty((4, SEQ, H), np.float32)
    for c in range(8):
        b, h = c // 2, c % 2
        out[b, h * TOK:(h + 1) * TOK, :] = res.results[c]["out_loc"]
    return out
